# version 01
# baseline (speedup 1.0000x reference)
import contextlib
import numpy as np
import ml_dtypes
import concourse.bass as bass
import concourse.mybir as mybir
from concourse.bass_utils import run_bass_kernel_spmd

F32 = mybir.dt.float32
BF16 = mybir.dt.bfloat16
FP8 = mybir.dt.float8e4
I32 = mybir.dt.int32
U8 = mybir.dt.uint8
AF = mybir.ActivationFunctionType
ALU = mybir.AluOpType
AX = mybir.AxisListType

NCORES = 8
D = 2048
SEQ = 2048
TPC = 1024
NF = D // 128
NT = TPC // 128
EPS = 1e-5
FP8MAX = 448.0
INTER1 = 5504
NI1 = INTER1 // 128
FFI = 8192
NFI = FFI // 128
SM_SCALE0 = float(np.float32(1.0) / np.sqrt(np.float32(128.0)))
SM_SCALE1 = float(np.float32(1.0) / np.sqrt(np.float32(2048.0)))

ENGINES = ("sync", "scalar", "vector", "tensor", "gpsimd")


class Phaser:

    def __init__(self, nc):
        self.nc = nc
        self.phases = []

    def phase(self, **bodies):
        self.phases.append(bodies)

    def finish(self, ctx):
        bar = ctx.enter_context(self.nc.semaphore("bar"))
        block = ctx.enter_context(self.nc.Block())
        totals = []
        cum = 0
        for ph in self.phases:
            totals.append(cum)
            cum += sum(n for (_, n) in ph.values())
        final_total = cum

        def run_engine(eng, ename):
            for i, ph in enumerate(self.phases):
                if ename not in ph:
                    continue
                emit, n_decl = ph[ename]
                if totals[i] > 0:
                    eng.wait_ge(bar, totals[i])
                got = [0]

                def inc(instr, amt=1):
                    instr.then_inc(bar, amt)
                    got[0] += amt
                    return instr

                emit(eng, inc)
                assert got[0] == n_decl, (
                    f"phase {i} engine {ename}: declared {n_decl} incs, got {got[0]}")
            if ename == "gpsimd":
                eng.wait_ge(bar, final_total)

        for ename in ENGINES:
            getattr(block, ename)(lambda eng, ename=ename: run_engine(eng, ename))


def _dma_tiles(eng, inc, sb, dram, rows, cols, to_sbuf=True, row0=0, col0=0):
    n = rows // 128
    for k in range(n):
        s_ap = sb[:, col0 + cols * k: col0 + cols * (k + 1)]
        d_ap = dram[row0 + 128 * k: row0 + 128 * (k + 1), :]
        if to_sbuf:
            inc(eng.dma_start(out=s_ap, in_=d_ap), 16)
        else:
            inc(eng.dma_start(out=d_ap, in_=s_ap), 16)
    return n



def prog_ln(with_maxes):
    nc = bass.Bass("TRN2", target_bir_lowering=False)
    x = nc.dram_tensor("x", [TPC, D], F32, kind="ExternalInput").ap()
    identF = nc.dram_tensor("identF", [128, 128], F32, kind="ExternalInput").ap()
    if with_maxes:
        wfc_sh = nc.dram_tensor("wfc_sh", [256, FFI], F32, kind="ExternalInput").ap()
        wproj_sh = nc.dram_tensor("wproj_sh", [1024, D], F32, kind="ExternalInput").ap()
    xn = nc.dram_tensor("xn", [TPC, D], F32, kind="ExternalOutput").ap()
    dbg = nc.dram_tensor("dbg", [128, 6 * NT], F32, kind="ExternalOutput").ap()
    maxes = nc.dram_tensor("maxes", [3, 1], F32, kind="ExternalOutput").ap()

    with contextlib.ExitStack() as ctx:
        sX = ctx.enter_context(nc.sbuf_tensor("sX", [128, NT * D], F32))
        sXn = ctx.enter_context(nc.sbuf_tensor("sXn", [128, NT * D], F32))
        sjunk = ctx.enter_context(nc.sbuf_tensor("sjunk", [128, D], F32))
        swA = ctx.enter_context(nc.sbuf_tensor("swA", [128, 2 * FFI], F32))
        sid = ctx.enter_context(nc.sbuf_tensor("sid", [128, 128], F32))
        sms = ctx.enter_context(nc.sbuf_tensor("sms", [128, NT], F32))
        sqs = ctx.enter_context(nc.sbuf_tensor("sqs", [128, NT], F32))
        smean = ctx.enter_context(nc.sbuf_tensor("smean", [128, NT], F32))
        stmp = ctx.enter_context(nc.sbuf_tensor("stmp", [128, NT], F32))
        svar = ctx.enter_context(nc.sbuf_tensor("svar", [128, NT], F32))
        srstd = ctx.enter_context(nc.sbuf_tensor("srstd", [128, NT], F32))
        sbias = ctx.enter_context(nc.sbuf_tensor("sbias", [128, NT], F32))
        sxm = ctx.enter_context(nc.sbuf_tensor("sxm", [128, NT], F32))
        swm = ctx.enter_context(nc.sbuf_tensor("swm", [128, 16], F32))
        scols = ctx.enter_context(nc.sbuf_tensor("scols", [128, 3], F32))
        sred = ctx.enter_context(nc.sbuf_tensor("sred", [3, 128], F32))
        sout3 = ctx.enter_context(nc.sbuf_tensor("sout3", [3, 1], F32))
        seps = ctx.enter_context(nc.sbuf_tensor("seps", [128, 1], F32))
        pt3 = ctx.enter_context(nc.psum_tensor("pt3", [3, 128], F32))

        ph = Phaser(nc)

        def p0(eng, inc):
            inc(eng.dma_start(out=sid[:], in_=identF), 16)
            _dma_tiles(eng, inc, sX, x, TPC, D)
            if with_maxes:
                _dma_tiles(eng, inc, swA, wfc_sh, 256, FFI)
        ph.phase(sync=(p0, 16 * (1 + NT + (2 if with_maxes else 0))))

        def p1s(eng, inc):
            for k in range(NT):
                inc(eng.activation(out=sjunk[:], in_=sX[:, D * k:D * (k + 1)],
                                   func=AF.Square, accum_out=sqs[:, k:k + 1]))

        def p1v(eng, inc):
            eng.memset(seps[:], EPS)
            for k in range(NT):
                inc(eng.tensor_reduce(out=sms[:, k:k + 1],
                                      in_=sX[:, D * k:D * (k + 1)],
                                      axis=AX.X, op=ALU.add))
            if with_maxes:
                for j in range(2):
                    inc(eng.tensor_reduce(out=swm[:, j:j + 1],
                                          in_=swA[:, FFI * j:FFI * (j + 1)],
                                          axis=AX.X, op=ALU.max,
                                          apply_absolute_value=True))
        ph.phase(scalar=(p1s, NT), vector=(p1v, NT + (2 if with_maxes else 0)))

        def p2d(eng, inc):
            _dma_tiles(eng, inc, swA, wproj_sh, 1024, D)

        def p2v(eng, inc):
            inc(eng.tensor_scalar(out=smean[:], in0=sms[:], scalar1=1.0 / D,
                                  scalar2=None, op0=ALU.mult))
        bodies = {"vector": (p2v, 1)}
        if with_maxes:
            bodies["sync"] = (p2d, 16 * 8)
        ph.phase(**bodies)

        def p2b(eng, inc):
            inc(eng.activation(out=stmp[:], in_=smean[:], func=AF.Square))
        ph.phase(scalar=(p2b, 1))

        def p2c(eng, inc):
            inc(eng.scalar_tensor_tensor(out=svar[:], in0=sqs[:], scalar=1.0 / D,
                                         in1=stmp[:], op0=ALU.mult,
                                         op1=ALU.subtract))
        ph.phase(vector=(p2c, 1))

        def p3s(eng, inc):
            inc(eng.activation(out=srstd[:], in_=svar[:], func=AF.Sqrt, bias=seps[:, 0:1]))

        def p3v(eng, inc):
            for j in range(8):
                inc(eng.tensor_reduce(out=swm[:, 2 + j:3 + j],
                                      in_=swA[:, D * j:D * (j + 1)],
                                      axis=AX.X, op=ALU.max,
                                      apply_absolute_value=True))
        bodies = {"scalar": (p3s, 1)}
        if with_maxes:
            bodies["vector"] = (p3v, 8)
        ph.phase(**bodies)

        def p4a(eng, inc):
            inc(eng.reciprocal(out=srstd[:], in_=srstd[:]))
        ph.phase(vector=(p4a, 1))

        def p4b(eng, inc):
            inc(eng.scalar_tensor_tensor(out=sbias[:], in0=smean[:], scalar=-1.0,
                                         in1=srstd[:], op0=ALU.mult, op1=ALU.mult))
        ph.phase(vector=(p4b, 1))

        def p5(eng, inc):
            for k in range(NT):
                inc(eng.activation(out=sXn[:, D * k:D * (k + 1)],
                                   in_=sX[:, D * k:D * (k + 1)], func=AF.Identity,
                                   bias=sbias[:, k:k + 1], scale=srstd[:, k:k + 1]))
        ph.phase(scalar=(p5, NT))

        def p6(eng, inc):
            for k in range(NT):
                inc(eng.tensor_reduce(out=sxm[:, k:k + 1],
                                      in_=sXn[:, D * k:D * (k + 1)], axis=AX.X,
                                      op=ALU.max, apply_absolute_value=True))
        ph.phase(vector=(p6, NT))

        def p7(eng, inc):
            eng.tensor_reduce(out=scols[:, 0:1], in_=sxm[:, 0:NT], axis=AX.X,
                              op=ALU.max)
            if with_maxes:
                eng.tensor_reduce(out=scols[:, 1:2], in_=swm[:, 0:2], axis=AX.X,
                                  op=ALU.max)
                inc(eng.tensor_reduce(out=scols[:, 2:3], in_=swm[:, 2:10],
                                      axis=AX.X, op=ALU.max))
            else:
                inc(eng.memset(scols[:, 1:3], 0.0))
        ph.phase(vector=(p7, 1))

        def p8(eng, inc):
            inc(eng.transpose(pt3[:], scols[:], sid[:]))
        ph.phase(tensor=(p8, 1))

        def p9(eng, inc):
            inc(eng.tensor_copy(out=sred[:], in_=pt3[:]))
        ph.phase(vector=(p9, 1))

        def p10(eng, inc):
            inc(eng.tensor_reduce(out=sout3[:], in_=sred[:], axis=AX.X, op=ALU.max))
        ph.phase(vector=(p10, 1))

        def p11(eng, inc):
            _dma_tiles(eng, inc, sXn, xn, TPC, D, to_sbuf=False)
            inc(eng.dma_start(out=maxes, in_=sout3[:]), 16)
            for j, t in enumerate((sms, sqs, smean, svar, srstd, sbias)):
                inc(eng.dma_start(out=dbg[:, NT * j:NT * (j + 1)], in_=t[:, 0:NT]), 16)
        ph.phase(gpsimd=(p11, 16 * (NT + 7)))

        ph.finish(ctx)
    return nc



def prog_prep0():
    nc = bass.Bass("TRN2", target_bir_lowering=False)
    xn = nc.dram_tensor("xn", [TPC, D], F32, kind="ExternalInput").ap()
    allmax = nc.dram_tensor("allmax", [1, 24], F32, kind="ExternalInput").ap()
    wq = nc.dram_tensor("wq", [NF, D, 128], F32, kind="ExternalInput").ap()
    wk = nc.dram_tensor("wk", [NF, D, 128], F32, kind="ExternalInput").ap()
    wv = nc.dram_tensor("wv", [NF, D, 128], F32, kind="ExternalInput").ap()
    cosT = nc.dram_tensor("cosT", [TPC, TPC], F32, kind="ExternalInput").ap()
    sinT = nc.dram_tensor("sinT", [TPC, TPC], F32, kind="ExternalInput").ap()
    wfc_sh = nc.dram_tensor("wfc_sh", [256, FFI], F32, kind="ExternalInput").ap()
    wproj_sh = nc.dram_tensor("wproj_sh", [1024, D], F32, kind="ExternalInput").ap()
    identB = nc.dram_tensor("identB", [128, 128], BF16, kind="ExternalInput").ap()
    qf = nc.dram_tensor("qf", [D, TPC], BF16, kind="ExternalOutput").ap()
    kf = nc.dram_tensor("kf", [D, TPC], BF16, kind="ExternalOutput").ap()
    vf = nc.dram_tensor("vf", [D, TPC], BF16, kind="ExternalOutput").ap()
    x8f = nc.dram_tensor("x8f", [D, TPC], BF16, kind="ExternalOutput").ap()
    wfc8_sh = nc.dram_tensor("wfc8_sh", [256, FFI], BF16, kind="ExternalOutput").ap()
    wproj8_sh = nc.dram_tensor("wproj8_sh", [1024, D], BF16, kind="ExternalOutput").ap()

    with contextlib.ExitStack() as ctx:
        sXnF = ctx.enter_context(nc.sbuf_tensor("sXnF", [128, NF * TPC], BF16))
        sxt = ctx.enter_context(nc.sbuf_tensor("sxt", [128, 2 * D], F32))
        sxb = ctx.enter_context(nc.sbuf_tensor("sxb", [128, 2 * D], BF16))
        str_ = ctx.enter_context(nc.sbuf_tensor("str_", [128, 2 * 128], BF16))
        sam = ctx.enter_context(nc.sbuf_tensor("sam", [1, 24], F32))
        srow9 = ctx.enter_context(nc.sbuf_tensor("srow9", [1, 16], F32))
        sone = ctx.enter_context(nc.sbuf_tensor("sone", [1, 128], F32))
        scols9 = ctx.enter_context(nc.sbuf_tensor("scols9", [128, 12], F32))
        qf8a = ctx.enter_context(nc.sbuf_tensor("qf8a", [128, 2 * D], FP8))
        qf8b = ctx.enter_context(nc.sbuf_tensor("qf8b", [128, 2 * D], FP8))
        qua = ctx.enter_context(nc.sbuf_tensor("qua", [128, 2 * D], BF16))
        qub = ctx.enter_context(nc.sbuf_tensor("qub", [128, 2 * D], BF16))
        qab = ctx.enter_context(nc.sbuf_tensor("qab", [128, 2 * D], F32))
        qmk = ctx.enter_context(nc.sbuf_tensor("qmk", [128, 2 * D], U8))
        qmk2 = ctx.enter_context(nc.sbuf_tensor("qmk2", [128, 2 * D], U8))
        qout = ctx.enter_context(nc.sbuf_tensor("qout", [128, 2 * D], BF16))
        sid = ctx.enter_context(nc.sbuf_tensor("sid", [128, 128], BF16))
        swf = ctx.enter_context(nc.sbuf_tensor("swf", [128, 2 * D], F32))
        swb = ctx.enter_context(nc.sbuf_tensor("swb", [128, 2 * D], BF16))
        scs = ctx.enter_context(nc.sbuf_tensor("scs", [128, TPC], F32))
        ssn = ctx.enter_context(nc.sbuf_tensor("ssn", [128, TPC], F32))
        st1 = ctx.enter_context(nc.sbuf_tensor("st1", [128, TPC], F32))
        st2 = ctx.enter_context(nc.sbuf_tensor("st2", [128, TPC], F32))
        sQa = ctx.enter_context(nc.sbuf_tensor("sQa", [128, TPC], BF16))
        sQb = ctx.enter_context(nc.sbuf_tensor("sQb", [128, TPC], BF16))
        ptr = ctx.enter_context(nc.psum_tensor("ptr", [128, 512], BF16))
        pa = ctx.enter_context(nc.psum_tensor("pa", [128, TPC], F32))
        pb = ctx.enter_context(nc.psum_tensor("pb", [128, TPC], F32))
        p9 = ctx.enter_context(nc.psum_tensor("p9", [128, 9], F32))

        ph = Phaser(nc)

        def p0(eng, inc):
            inc(eng.dma_start(out=sid[:], in_=identB), 16)
            inc(eng.dma_start(out=sam[:], in_=allmax), 16)
        ph.phase(sync=(p0, 32))

        def p1(eng, inc):
            eng.memset(sone[:], 1.0)
            for j, (base, cnt) in enumerate(((0, 8), (8, 8), (16, 8))):
                inc(eng.tensor_reduce(out=srow9[0:1, 9 + j:10 + j],
                                      in_=sam[0:1, base:base + cnt],
                                      axis=AX.X, op=ALU.max))
        ph.phase(vector=(p1, 3))

        def p2(eng, inc):
            for j in range(3):
                g = srow9[0:1, 9 + j:10 + j]
                sc = srow9[0:1, 12 + j:13 + j]
                inc(eng.tensor_scalar(out=sc, in0=g, scalar1=1.0 / FP8MAX,
                                      scalar2=1e-6, op0=ALU.mult, op1=ALU.max))
        ph.phase(vector=(p2, 3))

        def p2x(eng, inc):
            for j in range(3):
                sc = srow9[0:1, 12 + j:13 + j]
                inc(eng.tensor_scalar(out=srow9[0:1, 3 * j + 2:3 * j + 3], in0=sc,
                                      scalar1=240.0, scalar2=None, op0=ALU.mult))
        ph.phase(vector=(p2x, 3))

        def p3(eng, inc):
            for j in range(3):
                sc = srow9[0:1, 12 + j:13 + j]
                inc(eng.reciprocal(out=srow9[0:1, 3 * j:3 * j + 1], in_=sc))
        ph.phase(vector=(p3, 3))

        def p4(eng, inc):
            for j in range(3):
                inc(eng.tensor_scalar(out=srow9[0:1, 3 * j + 1:3 * j + 2],
                                      in0=srow9[0:1, 3 * j:3 * j + 1],
                                      scalar1=0.5, scalar2=None, op0=ALU.mult))
        ph.phase(vector=(p4, 3))

        def p5(eng, inc):
            inc(eng.matmul(p9[:], sone[:], srow9[0:1, 0:9], start=True, stop=True))
        ph.phase(tensor=(p5, 1))

        def p6(eng, inc):
            inc(eng.tensor_copy(out=scols9[:, 0:9], in_=p9[:]))
        ph.phase(vector=(p6, 1))

        def p6b(eng, inc):
            for j in range(3):
                inc(eng.tensor_scalar(out=scols9[:, 9 + j:10 + j],
                                      in0=scols9[:, 3 * j + 2:3 * j + 3],
                                      scalar1=-1.0, scalar2=None, op0=ALU.mult))
        ph.phase(vector=(p6b, 3))

        def qc_of(j):
            return {"inv": scols9[:, 3 * j:3 * j + 1],
                    "invh": scols9[:, 3 * j + 1:3 * j + 2],
                    "thr": scols9[:, 3 * j + 2:3 * j + 3],
                    "nthr": scols9[:, 9 + j:10 + j]}

        for k in range(NT):
            s = k % 2

            def pl(eng, inc, k=k, s=s):
                inc(eng.dma_start(out=sxt[:, D * s:D * (s + 1)],
                                  in_=xn[128 * k:128 * (k + 1), :]), 16)
            ph.phase(sync=(pl, 16))

            def pc(eng, inc, s=s):
                inc(eng.activation(out=sxb[:, D * s:D * (s + 1)],
                                   in_=sxt[:, D * s:D * (s + 1)], func=AF.Copy))
            ph.phase(scalar=(pc, 1))

            def pt(eng, inc, k=k, s=s):
                for j in range(NF):
                    inc(eng.transpose(ptr[:, 128 * (j % 2):128 * (j % 2) + 128],
                                      sxb[:, D * s + 128 * j:D * s + 128 * (j + 1)],
                                      sid[:]))

            def pe(eng, inc, k=k):
                for j in range(NF):
                    inc(eng.tensor_copy(
                        out=sXnF[:, TPC * j + 128 * k:TPC * j + 128 * (k + 1)],
                        in_=ptr[:, 128 * (j % 2):128 * (j % 2) + 128]))
            for j in range(NF):
                def ptj(eng, inc, k=k, s=s, j=j):
                    inc(eng.transpose(ptr[:, 128 * (j % 2):128 * (j % 2) + 128],
                                      sxb[:, D * s + 128 * j:D * s + 128 * (j + 1)],
                                      sid[:]))

                def pej(eng, inc, k=k, j=j):
                    inc(eng.tensor_copy(
                        out=sXnF[:, TPC * j + 128 * k:TPC * j + 128 * (k + 1)],
                        in_=ptr[:, 128 * (j % 2):128 * (j % 2) + 128]))
                ph.phase(tensor=(ptj, 1))
                ph.phase(vector=(pej, 1))

        for k in range(NT):
            s = k % 2

            def pl(eng, inc, k=k, s=s):
                inc(eng.dma_start(out=sxt[:, D * s:D * (s + 1)],
                                  in_=xn[128 * k:128 * (k + 1), :]), 16)
            ph.phase(sync=(pl, 16))

            def pact(eng, inc, s=s):
                src = sxt[:, D * s:D * (s + 1)]
                qc = qc_of(0)
                eng.activation(out=qf8a[:, 0:D], in_=src, func=AF.Identity, scale=qc["inv"])
                inc(eng.activation(out=qua[:, 0:D], in_=qf8a[:, 0:D], func=AF.Copy))
                eng.activation(out=qf8b[:, 0:D], in_=src, func=AF.Identity, scale=qc["invh"])
                inc(eng.activation(out=qub[:, 0:D], in_=qf8b[:, 0:D], func=AF.Copy,
                                   scale=2.0))
            ph.phase(scalar=(pact, 2))

            def pm1(eng, inc, s=s):
                src = sxt[:, D * s:D * (s + 1)]
                qc = qc_of(0)
                inc(eng.tensor_scalar(out=qmk[:, 0:D], in0=src,
                                      scalar1=qc["thr"], scalar2=None, op0=ALU.is_ge))
                inc(eng.tensor_scalar(out=qmk2[:, 0:D], in0=src,
                                      scalar1=qc["nthr"], scalar2=None, op0=ALU.is_le))
            ph.phase(vector=(pm1, 2))

            def pm2(eng, inc):
                inc(eng.tensor_tensor(out=qmk[:, 0:D], in0=qmk[:, 0:D],
                                      in1=qmk2[:, 0:D], op=ALU.logical_or))
            ph.phase(vector=(pm2, 1))

            def psel(eng, inc):
                inc(eng.select(out=qout[:, 0:D], mask=qmk[:, 0:D],
                               on_true=qub[:, 0:D], on_false=qua[:, 0:D]))
            ph.phase(vector=(psel, 1))

            for j in range(NF):
                def ptj(eng, inc, j=j):
                    inc(eng.transpose(ptr[:, 128 * (j % 2):128 * (j % 2) + 128],
                                      qout[:, 128 * j:128 * (j + 1)], sid[:]))

                def pej(eng, inc, j=j):
                    inc(eng.tensor_copy(out=str_[:, 128 * (j % 2):128 * (j % 2) + 128],
                                        in_=ptr[:, 128 * (j % 2):128 * (j % 2) + 128]))

                def psj(eng, inc, k=k, j=j):
                    inc(eng.dma_start(
                        out=x8f[128 * j:128 * (j + 1), 128 * k:128 * (k + 1)],
                        in_=str_[:, 128 * (j % 2):128 * (j % 2) + 128]), 16)
                ph.phase(tensor=(ptj, 1))
                ph.phase(vector=(pej, 1))
                ph.phase(gpsimd=(psj, 16))

        wq_jobs = []
        for t in range(2):
            for cchunk in range(4):
                wq_jobs.append((wfc_sh, wfc8_sh, 128 * t, D * cchunk, 1))
        for t in range(8):
            wq_jobs.append((wproj_sh, wproj8_sh, 128 * t, 0, 2))
        for (srcd, dstd, r0, c0, qi) in wq_jobs:
            def pl(eng, inc, srcd=srcd, r0=r0, c0=c0):
                inc(eng.dma_start(out=sxt[:, 0:D],
                                  in_=srcd[r0:r0 + 128, c0:c0 + D]), 16)
            ph.phase(sync=(pl, 16))

            def pact(eng, inc, qi=qi):
                src = sxt[:, 0:D]
                qc = qc_of(qi)
                eng.activation(out=qf8a[:, 0:D], in_=src, func=AF.Identity, scale=qc["inv"])
                inc(eng.activation(out=qua[:, 0:D], in_=qf8a[:, 0:D], func=AF.Copy))
                eng.activation(out=qf8b[:, 0:D], in_=src, func=AF.Identity, scale=qc["invh"])
                inc(eng.activation(out=qub[:, 0:D], in_=qf8b[:, 0:D], func=AF.Copy,
                                   scale=2.0))
            ph.phase(scalar=(pact, 2))

            def pm1(eng, inc, qi=qi):
                src = sxt[:, 0:D]
                qc = qc_of(qi)
                inc(eng.tensor_scalar(out=qmk[:, 0:D], in0=src,
                                      scalar1=qc["thr"], scalar2=None, op0=ALU.is_ge))
                inc(eng.tensor_scalar(out=qmk2[:, 0:D], in0=src,
                                      scalar1=qc["nthr"], scalar2=None, op0=ALU.is_le))
            ph.phase(vector=(pm1, 2))

            def pm2(eng, inc):
                inc(eng.tensor_tensor(out=qmk[:, 0:D], in0=qmk[:, 0:D],
                                      in1=qmk2[:, 0:D], op=ALU.logical_or))
            ph.phase(vector=(pm2, 1))

            def psel(eng, inc):
                inc(eng.select(out=qout[:, 0:D], mask=qmk[:, 0:D],
                               on_true=qub[:, 0:D], on_false=qua[:, 0:D]))
            ph.phase(vector=(psel, 1))

            def pst(eng, inc, dstd=dstd, r0=r0, c0=c0):
                inc(eng.dma_start(out=dstd[r0:r0 + 128, c0:c0 + D],
                                  in_=qout[:, 0:D]), 16)
            ph.phase(gpsimd=(pst, 16))

        for mp in range(8):
            def plcs(eng, inc, mp=mp):
                inc(eng.dma_start(out=scs[:], in_=cosT[128 * mp:128 * (mp + 1), :]), 16)
                inc(eng.dma_start(out=ssn[:], in_=sinT[128 * mp:128 * (mp + 1), :]), 16)
            ph.phase(sync=(plcs, 32))

            for wi, (wten, outd, rope) in enumerate(
                    ((wq, qf, True), (wk, kf, True), (wv, vf, False))):
                def plw(eng, inc, wten=wten, mp=mp):
                    for half, m in enumerate((mp, mp + 8)):
                        for j in range(NF):
                            inc(eng.dma_start(
                                out=swf[:, D * half + 128 * j:D * half + 128 * (j + 1)],
                                in_=wten[m, 128 * j:128 * (j + 1), :]), 16)
                ph.phase(sync=(plw, 16 * 32))

                def pcw(eng, inc):
                    inc(eng.activation(out=swb[:, 0:D], in_=swf[:, 0:D], func=AF.Copy))
                    inc(eng.activation(out=swb[:, D:2 * D], in_=swf[:, D:2 * D],
                                       func=AF.Copy))
                ph.phase(scalar=(pcw, 2))

                def pmm(eng, inc):
                    for half, pdst in enumerate((pa, pb)):
                        for h in range(2):
                            for j in range(NF):
                                mm = eng.matmul(
                                    pdst[:, 512 * h:512 * (h + 1)],
                                    swb[:, D * half + 128 * j:D * half + 128 * (j + 1)],
                                    sXnF[:, TPC * j + 512 * h:TPC * j + 512 * (h + 1)],
                                    start=(j == 0), stop=(j == NF - 1))
                            inc(mm)
                ph.phase(tensor=(pmm, 4))

                if rope:
                    def prp(eng, inc):
                        eng.tensor_tensor(out=st1[:], in0=scs[:], in1=pa[:], op=ALU.mult)
                        eng.tensor_tensor(out=st2[:], in0=ssn[:], in1=pb[:], op=ALU.mult)
                        inc(eng.tensor_tensor(out=sQa[:], in0=st1[:], in1=st2[:],
                                              op=ALU.subtract))
                        eng.tensor_tensor(out=st1[:], in0=scs[:], in1=pb[:], op=ALU.mult)
                        eng.tensor_tensor(out=st2[:], in0=ssn[:], in1=pa[:], op=ALU.mult)
                        inc(eng.tensor_tensor(out=sQb[:], in0=st1[:], in1=st2[:],
                                              op=ALU.add))
                    ph.phase(vector=(prp, 2))
                else:
                    def prp(eng, inc):
                        inc(eng.tensor_copy(out=sQa[:], in_=pa[:]))
                        inc(eng.tensor_copy(out=sQb[:], in_=pb[:]))
                    ph.phase(vector=(prp, 2))

                def pso(eng, inc, outd=outd, mp=mp):
                    inc(eng.dma_start(out=outd[128 * mp:128 * (mp + 1), :],
                                      in_=sQa[:]), 16)
                    inc(eng.dma_start(out=outd[128 * (mp + 8):128 * (mp + 9), :],
                                      in_=sQb[:]), 16)
                ph.phase(gpsimd=(pso, 32))

        ph.finish(ctx)
    return nc



def host_constants():
    c = {}
    c["identB"] = np.eye(128, dtype=np.float32).astype(ml_dtypes.bfloat16)
    c["identF"] = np.eye(128, dtype=np.float32)
    H = np.int32(D)
    inv_freq = (np.float32(1.0) /
                np.power(np.float32(10000.0),
                         np.arange(0, D, 2, dtype=np.float32) / np.float32(H)))
    t = np.arange(SEQ, dtype=np.float32)
    freqs = t[:, None] * inv_freq[None, :]
    cos = np.cos(freqs).astype(np.float32)
    sin = np.sin(freqs).astype(np.float32)
    c["cosT"] = [np.ascontiguousarray(cos[h * TPC:(h + 1) * TPC, :].T) for h in range(2)]
    c["sinT"] = [np.ascontiguousarray(sin[h * TPC:(h + 1) * TPC, :].T) for h in range(2)]
    return c


def _tile_wT(w):
    o, i = w.shape
    return np.ascontiguousarray(w.T.reshape(i, o // 128, 128).transpose(1, 0, 2))


def host_shards(inputs):
    d = {}
    x = np.ascontiguousarray(inputs["L0_attn_hidden_states"]).reshape(8192, D)
    d["x_slab"] = [np.ascontiguousarray(x[TPC * c:TPC * (c + 1)]) for c in range(NCORES)]
    wfcT = np.ascontiguousarray(inputs["L0_ffn_w_fc"].T)
    wprojT = np.ascontiguousarray(inputs["L0_ffn_w_proj"].T)
    d["wfcT_sh"] = [np.ascontiguousarray(wfcT[256 * c:256 * (c + 1)]) for c in range(NCORES)]
    d["wprojT_sh"] = [np.ascontiguousarray(wprojT[1024 * c:1024 * (c + 1)]) for c in range(NCORES)]
    for nm, key in (("wq0", "L0_attn_Wq"), ("wk0", "L0_attn_Wk"), ("wv0", "L0_attn_Wv"),
                    ("wo0", "L0_attn_Wo"), ("wq1", "L1_attn_Wq"), ("wk1", "L1_attn_Wk"),
                    ("wv1", "L1_attn_Wv"), ("wo1", "L1_attn_Wo")):
        d[nm] = _tile_wT(np.ascontiguousarray(inputs[key]))
    d["gate_row"] = np.ascontiguousarray(inputs["L0_attn_gate"]).reshape(1, 16)
    d["woT0"] = np.ascontiguousarray(inputs["L0_attn_Wo"].T)
    d["woT1"] = np.ascontiguousarray(inputs["L1_attn_Wo"].T)
    d["bo_row"] = np.ascontiguousarray(inputs["L1_attn_bo"]).reshape(1, D)
    d["boT"] = np.ascontiguousarray(inputs["L1_attn_bo"].reshape(16, 128).T)
    d["lut_row"] = np.ascontiguousarray(inputs["L1_ffn_lut"]).reshape(1, 16)
    gT = np.ascontiguousarray(inputs["L1_ffn_gate_idx"].T)
    uT = np.ascontiguousarray(inputs["L1_ffn_up_idx"].T)
    dT = np.ascontiguousarray(inputs["L1_ffn_down_idx"].T)
    d["gidxT_sh"] = [np.ascontiguousarray(gT[256 * c:256 * (c + 1)]) for c in range(NCORES)]
    d["uidxT_sh"] = [np.ascontiguousarray(uT[256 * c:256 * (c + 1)]) for c in range(NCORES)]
    d["didxT_sh"] = [np.ascontiguousarray(dT[688 * c:688 * (c + 1)]) for c in range(NCORES)]
    return d


def run_ln(shards, consts, with_maxes=True):
    nc = _cached("ln", prog_ln, with_maxes)
    in_maps = []
    for c in range(NCORES):
        m = {"x": shards["x_slab"][c], "identF": consts["identF"]}
        if with_maxes:
            m["wfc_sh"] = shards["wfcT_sh"][c]
            m["wproj_sh"] = shards["wprojT_sh"][c]
        in_maps.append(m)
    res = run_bass_kernel_spmd(nc, in_maps, core_ids=list(range(NCORES)))
    xn = [res.results[c]["xn"] for c in range(NCORES)]
    maxes = [res.results[c]["maxes"] for c in range(NCORES)]
    run_ln.dbg = [res.results[c]["dbg"] for c in range(NCORES)]
    allmax = np.concatenate([np.concatenate([maxes[c][j] for c in range(NCORES)])
                             for j in range(3)]).reshape(1, 24)
    return xn, allmax


def run_prep0(shards, consts, xn, allmax):
    nc = _cached("prep0", prog_prep0)
    in_maps = []
    for c in range(NCORES):
        half = c % 2
        in_maps.append({
            "xn": xn[c], "allmax": allmax,
            "wq": shards["wq0"], "wk": shards["wk0"], "wv": shards["wv0"],
            "cosT": consts["cosT"][half], "sinT": consts["sinT"][half],
            "wfc_sh": shards["wfcT_sh"][c], "wproj_sh": shards["wprojT_sh"][c],
            "identB": consts["identB"],
        })
    res = run_bass_kernel_spmd(nc, in_maps, core_ids=list(range(NCORES)))
    out = {}
    for k in ("qf", "kf", "vf", "x8f", "wfc8_sh", "wproj8_sh"):
        out[k] = [res.results[c][k] for c in range(NCORES)]
    return out


_PROG_CACHE = {}


def _cached(name, fn, *args):
    key = (name,) + args
    if key not in _PROG_CACHE:
        _PROG_CACHE[key] = fn(*args)
    return _PROG_CACHE[key]



def prog_attn0():
    nc = bass.Bass("TRN2", target_bir_lowering=False)
    qh = nc.dram_tensor("qh", [TPC, SEQ], BF16, kind="ExternalInput").ap()
    kh = nc.dram_tensor("kh", [TPC, SEQ], BF16, kind="ExternalInput").ap()
    vh = nc.dram_tensor("vh", [TPC, SEQ], BF16, kind="ExternalInput").ap()
    gate_row = nc.dram_tensor("gate_row", [1, 16], F32, kind="ExternalInput").ap()
    identB = nc.dram_tensor("identB", [128, 128], BF16, kind="ExternalInput").ap()
    identF = nc.dram_tensor("identF", [128, 128], F32, kind="ExternalInput").ap()
    comb = nc.dram_tensor("comb", [TPC, SEQ], BF16, kind="ExternalOutput").ap()

    NH = NHEADS = 8
    with contextlib.ExitStack() as ctx:
        sQH = ctx.enter_context(nc.sbuf_tensor("sQH", [128, NH * SEQ], BF16))
        sKH = ctx.enter_context(nc.sbuf_tensor("sKH", [128, NH * SEQ], BF16))
        sVH = ctx.enter_context(nc.sbuf_tensor("sVH", [128, NH * SEQ], BF16))
        sCMB = ctx.enter_context(nc.sbuf_tensor("sCMB", [128, NH * SEQ], BF16))
        sidB = ctx.enter_context(nc.sbuf_tensor("sidB", [128, 128], BF16))
        sidF = ctx.enter_context(nc.sbuf_tensor("sidF", [128, 128], F32))
        sqsc = ctx.enter_context(nc.sbuf_tensor("sqsc", [128, SEQ], BF16))
        se1 = ctx.enter_context(nc.sbuf_tensor("se1", [128, SEQ], F32))
        se2 = ctx.enter_context(nc.sbuf_tensor("se2", [128, SEQ], F32))
        se3 = ctx.enter_context(nc.sbuf_tensor("se3", [128, SEQ], F32))
        se4 = ctx.enter_context(nc.sbuf_tensor("se4", [128, SEQ], F32))
        sqf = ctx.enter_context(nc.sbuf_tensor("sqf", [128, SEQ], BF16))
        skf = ctx.enter_context(nc.sbuf_tensor("skf", [128, SEQ], BF16))
        skfT = ctx.enter_context(nc.sbuf_tensor("skfT", [128, SEQ], BF16))
        svT = ctx.enter_context(nc.sbuf_tensor("svT", [128, 16 * 129], BF16))
        snt = ctx.enter_context(nc.sbuf_tensor("snt", [128, 1], F32))
        smem = ctx.enter_context(nc.sbuf_tensor("smem", [128, 129], BF16))
        sneg = ctx.enter_context(nc.sbuf_tensor("sneg", [128, 16], F32))
        snegrb = ctx.enter_context(nc.sbuf_tensor("snegrb", [16, 128], BF16))
        snegrow = ctx.enter_context(nc.sbuf_tensor("snegrow", [1, SEQ], BF16))
        sonerow = ctx.enter_context(nc.sbuf_tensor("sonerow", [1, 128], BF16))
        sonerowF = ctx.enter_context(nc.sbuf_tensor("sonerowF", [1, 128], F32))
        sPT = ctx.enter_context(nc.sbuf_tensor("sPT", [128, 16 * 512], BF16))
        sgr = ctx.enter_context(nc.sbuf_tensor("sgr", [1, 32], F32))
        sgc = ctx.enter_context(nc.sbuf_tensor("sgc", [128, 16], F32))
        sg1c = ctx.enter_context(nc.sbuf_tensor("sg1c", [128, 16], F32))
        srA = ctx.enter_context(nc.sbuf_tensor("srA", [128, 4], F32))
        srM = ctx.enter_context(nc.sbuf_tensor("srM", [128, 4], F32))
        stA = ctx.enter_context(nc.sbuf_tensor("stA", [128, 128], F32))
        stM = ctx.enter_context(nc.sbuf_tensor("stM", [128, 128], F32))
        sCT = ctx.enter_context(nc.sbuf_tensor("sCT", [128, 128], BF16))
        pSA = ctx.enter_context(nc.psum_tensor("pSA", [128, 2048], F32))
        pSB = ctx.enter_context(nc.psum_tensor("pSB", [128, 1536], F32))
        pCT = ctx.enter_context(nc.psum_tensor("pCT", [128, 512], BF16))

        pSt = [pSA[:, 0:512], pSA[:, 512:1024]]
        pAV = pSB[:, 0:129]
        pMO = pSB[:, 512:641]

        ph = Phaser(nc)

        def p0(eng, inc):
            inc(eng.dma_start(out=sidB[:], in_=identB), 16)
            inc(eng.dma_start(out=sidF[:], in_=identF), 16)
            inc(eng.dma_start(out=sgr[0:1, 0:16], in_=gate_row), 16)
            _dma_tiles(eng, inc, sQH, qh, TPC, SEQ)
            _dma_tiles(eng, inc, sKH, kh, TPC, SEQ)
            _dma_tiles(eng, inc, sVH, vh, TPC, SEQ)
        ph.phase(sync=(p0, 16 * (3 + 24)))

        def pg1(eng, inc):
            inc(eng.activation(out=sgr[0:1, 16:32], in_=sgr[0:1, 0:16],
                               func=AF.Sigmoid))
        ph.phase(scalar=(pg1, 1))

        def pg2(eng, inc):
            inc(eng.tensor_scalar(out=sgr[0:1, 0:16], in0=sgr[0:1, 16:32],
                                  scalar1=-1.0, scalar2=1.0, op0=ALU.mult,
                                  op1=ALU.add))
            for j in range(16):
                eng.memset(svT[:, 129 * j + 128:129 * (j + 1)], 1.0)
            eng.memset(sonerow[:], 1.0)
            inc(eng.memset(sonerowF[:], 1.0))
        ph.phase(vector=(pg2, 2))

        def pg3(eng, inc):
            eng.matmul(pSA[0:128, 0:16], sonerowF[:], sgr[0:1, 16:32],
                       start=True, stop=True)
            inc(eng.matmul(pSA[0:128, 16:32], sonerowF[:], sgr[0:1, 0:16],
                           start=True, stop=True))
        ph.phase(tensor=(pg3, 1))

        def pg4(eng, inc):
            eng.tensor_copy(out=sgc[:], in_=pSA[0:128, 0:16])
            inc(eng.tensor_copy(out=sg1c[:], in_=pSA[0:128, 16:32]))
        ph.phase(vector=(pg4, 1))

        for i in range(NH):
            qi = sQH[:, SEQ * i:SEQ * (i + 1)]
            ki = sKH[:, SEQ * i:SEQ * (i + 1)]
            vi = sVH[:, SEQ * i:SEQ * (i + 1)]

            def a1(eng, inc, qi=qi, ki=ki):
                inc(eng.activation(out=sqsc[:], in_=qi, func=AF.Identity,
                                   scale=SM_SCALE0))
            ph.phase(scalar=(a1, 1))

            def a2(eng, inc, qi=qi, ki=ki):
                inc(eng.tensor_scalar(out=se1[:], in0=qi, scalar1=0.0,
                                      scalar2=None, op0=ALU.min))
                inc(eng.tensor_scalar(out=se2[:], in0=qi, scalar1=0.0,
                                      scalar2=None, op0=ALU.max))
                inc(eng.tensor_scalar(out=se3[:], in0=ki, scalar1=0.0,
                                      scalar2=None, op0=ALU.min))
                inc(eng.tensor_scalar(out=se4[:], in0=ki, scalar1=0.0,
                                      scalar2=None, op0=ALU.max))
            ph.phase(vector=(a2, 4))

            def a3(eng, inc):
                inc(eng.activation(out=se1[:], in_=se1[:], func=AF.Exp))
                inc(eng.activation(out=se3[:], in_=se3[:], func=AF.Exp))
            ph.phase(scalar=(a3, 2))

            def a4(eng, inc):
                inc(eng.tensor_add(out=sqf[:], in0=se1[:], in1=se2[:]))
                inc(eng.tensor_add(out=skf[:], in0=se3[:], in1=se4[:]))
            ph.phase(vector=(a4, 2))

            def a5(eng, inc):
                inc(eng.tensor_reduce(out=snt[:], in_=skf[:], axis=AX.X, op=ALU.add))
            ph.phase(vector=(a5, 1))

            def a6(eng, inc):
                inc(eng.tensor_copy(out=smem[:, 128:129], in_=snt[:]))
            ph.phase(vector=(a6, 1))

            for g4 in range(4):
                def b1(eng, inc, g4=g4, vi=vi):
                    for u in range(4):
                        j = 4 * g4 + u
                        inc(eng.transpose(pCT[:, 128 * u:128 * (u + 1)],
                                          skf[:, 128 * j:128 * (j + 1)], sidB[:]))
                ph.phase(tensor=(b1, 4))

                def b2(eng, inc, g4=g4):
                    for u in range(4):
                        j = 4 * g4 + u
                        inc(eng.tensor_copy(out=skfT[:, 128 * j:128 * (j + 1)],
                                            in_=pCT[:, 128 * u:128 * (u + 1)]))
                ph.phase(vector=(b2, 4))
            for g4 in range(4):
                def b3(eng, inc, g4=g4, vi=vi):
                    for u in range(4):
                        j = 4 * g4 + u
                        inc(eng.transpose(pCT[:, 128 * u:128 * (u + 1)],
                                          vi[:, 128 * j:128 * (j + 1)], sidB[:]))
                ph.phase(tensor=(b3, 4))

                def b4(eng, inc, g4=g4):
                    for u in range(4):
                        j = 4 * g4 + u
                        inc(eng.tensor_copy(out=svT[:, 129 * j:129 * j + 128],
                                            in_=pCT[:, 128 * u:128 * (u + 1)]))
                ph.phase(vector=(b4, 4))

            def c1(eng, inc):
                for j in range(16):
                    mm = eng.matmul(pSB[:, 1024:1152], skfT[:, 128 * j:128 * (j + 1)],
                                    svT[:, 129 * j:129 * j + 128],
                                    start=(j == 0), stop=(j == 15))
                inc(mm)
            ph.phase(tensor=(c1, 1))

            def c2(eng, inc):
                inc(eng.activation(out=smem[:, 0:128], in_=pSB[:, 1024:1152],
                                   func=AF.Copy))
            ph.phase(scalar=(c2, 1))

            for u in range(16):
                def d1(eng, inc, u=u, ki=ki):
                    for cc4 in range(4):
                        mm = eng.matmul(pSA[:, 512 * cc4:512 * (cc4 + 1)],
                                        sqsc[:, 128 * u:128 * (u + 1)],
                                        ki[:, 512 * cc4:512 * (cc4 + 1)],
                                        start=True, stop=True)
                    inc(mm)
                ph.phase(tensor=(d1, 1))

                def d2(eng, inc, u=u):
                    inc(eng.tensor_reduce(out=sneg[:, u:u + 1], in_=pSA[:],
                                          axis=AX.X, op=ALU.max, negate=True))
                ph.phase(vector=(d2, 1))

            def e1(eng, inc):
                inc(eng.transpose(pSA[0:16, 0:128], sneg[:], sidF[:]))
            ph.phase(tensor=(e1, 1))

            def e2(eng, inc):
                inc(eng.tensor_copy(out=snegrb[:], in_=pSA[0:16, 0:128]))
            ph.phase(vector=(e2, 1))

            def e3(eng, inc):
                for r in range(16):
                    inc(eng.dma_start(out=snegrow[0:1, 128 * r:128 * (r + 1)],
                                      in_=snegrb[r:r + 1, :]), 16)
            ph.phase(gpsimd=(e3, 256))

            for qb in range(4):
                qs = slice(512 * qb, 512 * (qb + 1))
                for kb in range(16):
                    def f1(eng, inc, kb=kb, qb=qb, ki=ki):
                        p = pSt[kb % 2]
                        eng.matmul(p, ki[:, 128 * kb:128 * (kb + 1)],
                                   sqsc[:, 512 * qb:512 * (qb + 1)],
                                   start=True, stop=False)
                        inc(eng.matmul(p, sonerow[:],
                                       snegrow[0:1, 512 * qb:512 * (qb + 1)],
                                       start=False, stop=True))
                    ph.phase(tensor=(f1, 1))

                    def f2(eng, inc, kb=kb):
                        inc(eng.activation(out=sPT[:, 512 * kb:512 * (kb + 1)],
                                           in_=pSt[kb % 2], func=AF.Exp))
                    ph.phase(scalar=(f2, 1))

                for v4 in range(4):
                    u = 4 * qb + v4

                    def f3(eng, inc, qb=qb, v4=v4, i=i):
                        for kb in range(16):
                            mm = eng.matmul(
                                pAV, sPT[:, 512 * kb + 128 * v4:512 * kb + 128 * (v4 + 1)],
                                svT[:, 129 * kb:129 * (kb + 1)],
                                start=(kb == 0), stop=(kb == 15))
                        inc(mm)
                        inc(eng.matmul(pMO, sqf[:, 512 * qb + 128 * v4:
                                                512 * qb + 128 * (v4 + 1)],
                                       smem[:, 0:129], start=True, stop=True))
                    ph.phase(tensor=(f3, 2))

                    def f4(eng, inc, v4=v4):
                        inc(eng.reciprocal(out=srA[:, v4:v4 + 1], in_=pAV[:, 128:129]))
                        inc(eng.reciprocal(out=srM[:, v4:v4 + 1], in_=pMO[:, 128:129]))
                    ph.phase(vector=(f4, 2))

                    def f5(eng, inc, v4=v4, i=i):
                        inc(eng.tensor_tensor(out=srA[:, v4:v4 + 1],
                                              in0=srA[:, v4:v4 + 1],
                                              in1=sg1c[:, i:i + 1], op=ALU.mult))
                        inc(eng.tensor_tensor(out=srM[:, v4:v4 + 1],
                                              in0=srM[:, v4:v4 + 1],
                                              in1=sgc[:, i:i + 1], op=ALU.mult))
                    ph.phase(vector=(f5, 2))

                    def f6(eng, inc, v4=v4):
                        inc(eng.activation(out=stA[:], in_=pAV[:, 0:128],
                                           func=AF.Identity, scale=srA[:, v4:v4 + 1]))
                        inc(eng.activation(out=stM[:], in_=pMO[:, 0:128],
                                           func=AF.Identity, scale=srM[:, v4:v4 + 1]))
                    ph.phase(scalar=(f6, 2))

                    def f7(eng, inc):
                        inc(eng.tensor_add(out=sCT[:], in0=stA[:], in1=stM[:]))
                    ph.phase(vector=(f7, 1))

                    def f8(eng, inc):
                        inc(eng.transpose(pCT[:, 0:128], sCT[:], sidB[:]))
                    ph.phase(tensor=(f8, 1))

                    def f9(eng, inc, i=i, qb=qb, v4=v4):
                        inc(eng.tensor_copy(
                            out=sCMB[:, SEQ * i + 512 * qb + 128 * v4:
                                     SEQ * i + 512 * qb + 128 * (v4 + 1)],
                            in_=pCT[:, 0:128]))
                    ph.phase(vector=(f9, 1))

        def pst(eng, inc):
            _dma_tiles(eng, inc, sCMB, comb, TPC, SEQ, to_sbuf=False)
        ph.phase(gpsimd=(pst, 16 * 8))

        ph.finish(ctx)
    return nc


def host_attn_shuffle(qkv):
    out = {}
    for nm in ("qf", "kf", "vf"):
        lst = []
        for c in range(NCORES):
            b = c // 2
            rows = slice(1024 * (c % 2), 1024 * (c % 2) + 1024)
            lst.append(np.ascontiguousarray(np.concatenate(
                [qkv[nm][2 * b][rows, :], qkv[nm][2 * b + 1][rows, :]], axis=1)))
        out[nm] = lst
    return out


def host_comb_shuffle(comb):
    lst = []
    for c in range(NCORES):
        b = c // 2
        cols = slice(1024 * (c % 2), 1024 * (c % 2) + 1024)
        lst.append(np.ascontiguousarray(np.concatenate(
            [comb[2 * b][:, cols], comb[2 * b + 1][:, cols]], axis=0)))
    return lst


def run_attn0(shards, consts, hshuf):
    nc = _cached("attn0", prog_attn0)
    in_maps = []
    for c in range(NCORES):
        gr = shards["gate_row"]
        if c % 2 == 1:
            gr = np.ascontiguousarray(np.concatenate([gr[:, 8:], gr[:, :8]], axis=1))
        in_maps.append({
            "qh": hshuf["qf"][c], "kh": hshuf["kf"][c], "vh": hshuf["vf"][c],
            "gate_row": gr,
            "identB": consts["identB"], "identF": consts["identF"],
        })
    res = run_bass_kernel_spmd(nc, in_maps, core_ids=list(range(NCORES)))
    return [res.results[c]["comb"] for c in range(NCORES)]


def add_tm2fm_phases(ph, src_dram, sXnF, sxt, sxb, ptr, sidB, ntiles=NT, cols=D):
    nf = cols // 128
    tpc = ntiles * 128
    for k in range(ntiles):
        s = k % 2

        def pl(eng, inc, k=k, s=s):
            inc(eng.dma_start(out=sxt[:, cols * s:cols * (s + 1)],
                              in_=src_dram[128 * k:128 * (k + 1), :]), 16)
        ph.phase(sync=(pl, 16))

        def pc(eng, inc, s=s):
            inc(eng.activation(out=sxb[:, cols * s:cols * (s + 1)],
                               in_=sxt[:, cols * s:cols * (s + 1)], func=AF.Copy))
        ph.phase(scalar=(pc, 1))

        for g4 in range(nf // 4):
            def pt4(eng, inc, g4=g4, s=s):
                for u in range(4):
                    j = 4 * g4 + u
                    inc(eng.transpose(ptr[:, 128 * u:128 * (u + 1)],
                                      sxb[:, cols * s + 128 * j:cols * s + 128 * (j + 1)],
                                      sidB[:]))
            ph.phase(tensor=(pt4, 4))

            def pe4(eng, inc, g4=g4, k=k):
                for u in range(4):
                    j = 4 * g4 + u
                    inc(eng.tensor_copy(
                        out=sXnF[:, tpc * j + 128 * k:tpc * j + 128 * (k + 1)],
                        in_=ptr[:, 128 * u:128 * (u + 1)]))
            ph.phase(vector=(pe4, 4))


def add_scale_col_phases(ph, sam, srow, scols, sone, pRep, specs):
    raise NotImplementedError



def prog_fc0():
    nc = bass.Bass("TRN2", target_bir_lowering=False)
    x8f = nc.dram_tensor("x8f", [D, TPC], BF16, kind="ExternalInput").ap()
    wfc8t = nc.dram_tensor("wfc8t", [NFI, D, 128], BF16, kind="ExternalInput").ap()
    allmax = nc.dram_tensor("allmax", [1, 24], F32, kind="ExternalInput").ap()
    identF = nc.dram_tensor("identF", [128, 128], F32, kind="ExternalInput").ap()
    hbuf = nc.dram_tensor("hbuf", [FFI, TPC], F32, kind="ExternalOutput").ap()
    hmax = nc.dram_tensor("hmax", [1, 1], F32, kind="ExternalOutput").ap()

    with contextlib.ExitStack() as ctx:
        sX8 = ctx.enter_context(nc.sbuf_tensor("sX8", [128, NF * TPC], BF16))
        sw = ctx.enter_context(nc.sbuf_tensor("sw", [128, 2 * D], BF16))
        sidF = ctx.enter_context(nc.sbuf_tensor("sidF", [128, 128], F32))
        srel = ctx.enter_context(nc.sbuf_tensor("srel", [128, TPC], F32))
        sh = ctx.enter_context(nc.sbuf_tensor("sh", [128, 2 * TPC], F32))
        sam = ctx.enter_context(nc.sbuf_tensor("sam", [1, 24], F32))
        srow = ctx.enter_context(nc.sbuf_tensor("srow", [1, 8], F32))
        sone = ctx.enter_context(nc.sbuf_tensor("sone", [1, 128], F32))
        sdq = ctx.enter_context(nc.sbuf_tensor("sdq", [128, 1], F32))
        shm = ctx.enter_context(nc.sbuf_tensor("shm", [128, NFI], F32))
        shm1 = ctx.enter_context(nc.sbuf_tensor("shm1", [128, 1], F32))
        shmr = ctx.enter_context(nc.sbuf_tensor("shmr", [1, 128], F32))
        sout1 = ctx.enter_context(nc.sbuf_tensor("sout1", [1, 1], F32))
        pacc = ctx.enter_context(nc.psum_tensor("pacc", [128, 2 * TPC], F32))
        pRep = ctx.enter_context(nc.psum_tensor("pRep", [128, 1], F32))

        ph = Phaser(nc)

        def p0(eng, inc):
            inc(eng.dma_start(out=sam[:], in_=allmax), 16)
            inc(eng.dma_start(out=sidF[:], in_=identF), 16)
            _dma_tiles(eng, inc, sX8, x8f, D, TPC)
        ph.phase(sync=(p0, 16 * 18))

        def p1(eng, inc):
            eng.memset(sone[:], 1.0)
            inc(eng.tensor_reduce(out=srow[0:1, 0:1], in_=sam[0:1, 0:8],
                                  axis=AX.X, op=ALU.max))
            inc(eng.tensor_reduce(out=srow[0:1, 1:2], in_=sam[0:1, 8:16],
                                  axis=AX.X, op=ALU.max))
        ph.phase(vector=(p1, 2))

        def p2(eng, inc):
            inc(eng.tensor_scalar(out=srow[0:1, 2:3], in0=srow[0:1, 0:1],
                                  scalar1=1.0 / FP8MAX, scalar2=1e-6,
                                  op0=ALU.mult, op1=ALU.max))
            inc(eng.tensor_scalar(out=srow[0:1, 3:4], in0=srow[0:1, 1:2],
                                  scalar1=1.0 / FP8MAX, scalar2=1e-6,
                                  op0=ALU.mult, op1=ALU.max))
        ph.phase(vector=(p2, 2))

        def p3(eng, inc):
            inc(eng.tensor_tensor(out=srow[0:1, 4:5], in0=srow[0:1, 2:3],
                                  in1=srow[0:1, 3:4], op=ALU.mult))
        ph.phase(vector=(p3, 1))

        def p4(eng, inc):
            inc(eng.matmul(pRep[:], sone[:], srow[0:1, 4:5], start=True, stop=True))
        ph.phase(tensor=(p4, 1))

        def p5(eng, inc):
            inc(eng.tensor_copy(out=sdq[:], in_=pRep[:]))
        ph.phase(vector=(p5, 1))

        for m in range(NFI):
            def pw(eng, inc, m=m):
                for j in range(NF):
                    inc(eng.dma_start(out=sw[:, 128 * j:128 * (j + 1)],
                                      in_=wfc8t[m, 128 * j:128 * (j + 1), :]), 16)
            ph.phase(sync=(pw, 16 * NF))

            def pm(eng, inc, m=m):
                for h in range(2):
                    for j in range(NF):
                        mm = eng.matmul(pacc[:, 512 * h:512 * (h + 1)],
                                        sw[:, 128 * j:128 * (j + 1)],
                                        sX8[:, TPC * j + 512 * h:TPC * j + 512 * (h + 1)],
                                        start=(j == 0), stop=(j == NF - 1))
                    inc(mm)
            ph.phase(tensor=(pm, 2))

            def pr(eng, inc, m=m):
                inc(eng.activation(out=srel[:], in_=pacc[:, 0:TPC], func=AF.Relu))
            ph.phase(scalar=(pr, 1))

            def pq(eng, inc, m=m):
                inc(eng.activation(out=sh[:, TPC * (m % 2):TPC * (m % 2 + 1)],
                                   in_=srel[:], func=AF.Square, scale=sdq[:, 0:1]))
            ph.phase(scalar=(pq, 1))

            def pv(eng, inc, m=m):
                inc(eng.tensor_reduce(out=shm[:, m:m + 1],
                                      in_=sh[:, TPC * (m % 2):TPC * (m % 2 + 1)],
                                      axis=AX.X, op=ALU.max))
            ph.phase(vector=(pv, 1))

            def ps(eng, inc, m=m):
                inc(eng.dma_start(out=hbuf[128 * m:128 * (m + 1), :],
                                  in_=sh[:, TPC * (m % 2):TPC * (m % 2 + 1)]), 16)
            ph.phase(gpsimd=(ps, 16))

        def pf1(eng, inc):
            inc(eng.tensor_reduce(out=shm1[:], in_=shm[:], axis=AX.X, op=ALU.max))
        ph.phase(vector=(pf1, 1))

        def pf2(eng, inc):
            inc(eng.transpose(pacc[0:1, 0:128], shm1[:], sidF[:]))
        ph.phase(tensor=(pf2, 1))

        def pf3(eng, inc):
            inc(eng.tensor_copy(out=shmr[:], in_=pacc[0:1, 0:128]))
        ph.phase(vector=(pf3, 1))

        def pf4(eng, inc):
            inc(eng.tensor_reduce(out=sout1[:], in_=shmr[:], axis=AX.X, op=ALU.max))
        ph.phase(vector=(pf4, 1))

        def pf5(eng, inc):
            inc(eng.dma_start(out=hmax, in_=sout1[:]), 16)
        ph.phase(gpsimd=(pf5, 16))

        ph.finish(ctx)
    return nc



def prog_proj0():
    nc = bass.Bass("TRN2", target_bir_lowering=False)
    x = nc.dram_tensor("x", [TPC, D], F32, kind="ExternalInput").ap()
    combd = nc.dram_tensor("combd", [D, TPC], BF16, kind="ExternalInput").ap()
    woT = nc.dram_tensor("woT", [D, D], F32, kind="ExternalInput").ap()
    hbuf = nc.dram_tensor("hbuf", [FFI, TPC], F32, kind="ExternalInput").ap()
    wproj8 = nc.dram_tensor("wproj8", [FFI, D], BF16, kind="ExternalInput").ap()
    allmax = nc.dram_tensor("allmax", [1, 24], F32, kind="ExternalInput").ap()
    allhmax = nc.dram_tensor("allhmax", [1, 8], F32, kind="ExternalInput").ap()
    x1 = nc.dram_tensor("x1", [TPC, D], F32, kind="ExternalOutput").ap()

    with contextlib.ExitStack() as ctx:
        sCMB = ctx.enter_context(nc.sbuf_tensor("sCMB", [128, NF * TPC], BF16))
        sR1 = ctx.enter_context(nc.sbuf_tensor("sR1", [128, NT * D], F32))
        swf = ctx.enter_context(nc.sbuf_tensor("swf", [128, 512], F32))
        swb = ctx.enter_context(nc.sbuf_tensor("swb", [128, 2 * 512], BF16))
        sxs = ctx.enter_context(nc.sbuf_tensor("sxs", [128, 512], F32))
        sam = ctx.enter_context(nc.sbuf_tensor("sam", [1, 32], F32))
        srow = ctx.enter_context(nc.sbuf_tensor("srow", [1, 16], F32))
        sone = ctx.enter_context(nc.sbuf_tensor("sone", [1, 128], F32))
        sqc = ctx.enter_context(nc.sbuf_tensor("sqc", [128, 8], F32))
        sht = ctx.enter_context(nc.sbuf_tensor("sht", [128, 2 * 512], F32))
        sh8 = ctx.enter_context(nc.sbuf_tensor("sh8", [128, NFI * 512], BF16))
        swp = ctx.enter_context(nc.sbuf_tensor("swp", [128, 2 * 512], BF16))
        qf8a = ctx.enter_context(nc.sbuf_tensor("qf8a", [128, TPC], FP8))
        qf8b = ctx.enter_context(nc.sbuf_tensor("qf8b", [128, TPC], FP8))
        qua = ctx.enter_context(nc.sbuf_tensor("qua", [128, TPC], BF16))
        qub = ctx.enter_context(nc.sbuf_tensor("qub", [128, TPC], BF16))
        qmk = ctx.enter_context(nc.sbuf_tensor("qmk", [128, TPC], U8))
        qmk2 = ctx.enter_context(nc.sbuf_tensor("qmk2", [128, TPC], U8))
        stp = ctx.enter_context(nc.sbuf_tensor("stp", [128, 512], F32))
        pW = ctx.enter_context(nc.psum_tensor("pW", [128, 8 * 512], F32))

        ph = Phaser(nc)

        def p0(eng, inc):
            inc(eng.dma_start(out=sam[0:1, 0:24], in_=allmax), 16)
            inc(eng.dma_start(out=sam[0:1, 24:32], in_=allhmax), 16)
            _dma_tiles(eng, inc, sCMB, combd, D, TPC)
        ph.phase(sync=(p0, 16 * 18))

        def p1(eng, inc):
            eng.memset(sone[:], 1.0)
            inc(eng.tensor_reduce(out=srow[0:1, 0:1], in_=sam[0:1, 24:32],
                                  axis=AX.X, op=ALU.max))
            inc(eng.tensor_reduce(out=srow[0:1, 1:2], in_=sam[0:1, 16:24],
                                  axis=AX.X, op=ALU.max))
        ph.phase(vector=(p1, 2))

        def p2(eng, inc):
            inc(eng.tensor_scalar(out=srow[0:1, 2:3], in0=srow[0:1, 0:1],
                                  scalar1=1.0 / FP8MAX, scalar2=1e-6,
                                  op0=ALU.mult, op1=ALU.max))
            inc(eng.tensor_scalar(out=srow[0:1, 3:4], in0=srow[0:1, 1:2],
                                  scalar1=1.0 / FP8MAX, scalar2=1e-6,
                                  op0=ALU.mult, op1=ALU.max))
        ph.phase(vector=(p2, 2))

        def p3(eng, inc):
            inc(eng.tensor_tensor(out=srow[0:1, 4:5], in0=srow[0:1, 2:3],
                                  in1=srow[0:1, 3:4], op=ALU.mult))
            inc(eng.reciprocal(out=srow[0:1, 5:6], in_=srow[0:1, 2:3]))
        ph.phase(vector=(p3, 2))

        def p4(eng, inc):
            inc(eng.tensor_scalar(out=srow[0:1, 6:7], in0=srow[0:1, 5:6],
                                  scalar1=0.5, scalar2=None, op0=ALU.mult))
            inc(eng.tensor_scalar(out=srow[0:1, 7:8], in0=srow[0:1, 2:3],
                                  scalar1=240.0, scalar2=None, op0=ALU.mult))
        ph.phase(vector=(p4, 2))

        def p4b(eng, inc):
            inc(eng.tensor_scalar(out=srow[0:1, 8:9], in0=srow[0:1, 7:8],
                                  scalar1=-1.0, scalar2=None, op0=ALU.mult))
        ph.phase(vector=(p4b, 1))

        def p5(eng, inc):
            inc(eng.matmul(pW[0:128, 0:5], sone[:], srow[0:1, 4:9],
                           start=True, stop=True))
        ph.phase(tensor=(p5, 1))

        def p6(eng, inc):
            inc(eng.tensor_copy(out=sqc[:, 0:5], in_=pW[0:128, 0:5]))
        ph.phase(vector=(p6, 1))
        DQ2, INVH, INVH2, THR, NTHR = (sqc[:, j:j + 1] for j in range(5))

        for cP in range(4):
            for j in range(NF):
                s = j % 2

                def pwl(eng, inc, cP=cP, j=j):
                    inc(eng.dma_start(out=swf[:],
                                      in_=woT[128 * j:128 * (j + 1),
                                              512 * cP:512 * (cP + 1)]), 16)
                ph.phase(sync=(pwl, 16))

                def pwc(eng, inc, s=s):
                    inc(eng.activation(out=swb[:, 512 * s:512 * (s + 1)],
                                       in_=swf[:], func=AF.Copy))
                ph.phase(scalar=(pwc, 1))

                def pwm(eng, inc, j=j, s=s):
                    for t in range(8):
                        mm = eng.matmul(pW[:, 512 * t:512 * (t + 1)],
                                        sCMB[:, TPC * j + 128 * t:
                                             TPC * j + 128 * (t + 1)],
                                        swb[:, 512 * s:512 * (s + 1)],
                                        start=(j == 0), stop=(j == NF - 1))
                    inc(mm)
                ph.phase(tensor=(pwm, 1))

            for t in range(8):
                def pxl(eng, inc, t=t, cP=cP):
                    inc(eng.dma_start(out=sxs[:],
                                      in_=x[128 * t:128 * (t + 1),
                                            512 * cP:512 * (cP + 1)]), 16)
                ph.phase(sync=(pxl, 16))

                def pev(eng, inc, t=t, cP=cP):
                    inc(eng.tensor_add(out=sR1[:, D * t + 512 * cP:
                                            D * t + 512 * (cP + 1)],
                                       in0=sxs[:], in1=pW[:, 512 * t:512 * (t + 1)]))
                ph.phase(vector=(pev, 1))

        for half in range(2):
            hs0 = 512 * half
            for j in range(NFI):
                s = j % 2

                def phl(eng, inc, j=j, hs0=hs0, s=s):
                    inc(eng.dma_start(out=sht[:, 512 * s:512 * (s + 1)],
                                      in_=hbuf[128 * j:128 * (j + 1),
                                               hs0:hs0 + 512]), 16)
                ph.phase(sync=(phl, 16))

                def pha(eng, inc, s=s):
                    src = sht[:, 512 * s:512 * (s + 1)]
                    eng.activation(out=qf8a[:, 0:512], in_=src, func=AF.Identity,
                                   scale=INVH)
                    inc(eng.activation(out=qua[:, 0:512], in_=qf8a[:, 0:512],
                                       func=AF.Copy))
                    eng.activation(out=qf8b[:, 0:512], in_=src, func=AF.Identity,
                                   scale=INVH2)
                    inc(eng.activation(out=qub[:, 0:512], in_=qf8b[:, 0:512],
                                       func=AF.Copy, scale=2.0))
                ph.phase(scalar=(pha, 2))

                def phm(eng, inc, s=s):
                    src = sht[:, 512 * s:512 * (s + 1)]
                    inc(eng.tensor_scalar(out=qmk[:, 0:512], in0=src, scalar1=THR,
                                          scalar2=None, op0=ALU.is_ge))
                    inc(eng.tensor_scalar(out=qmk2[:, 0:512], in0=src, scalar1=NTHR,
                                          scalar2=None, op0=ALU.is_le))
                ph.phase(vector=(phm, 2))

                def pho(eng, inc):
                    inc(eng.tensor_tensor(out=qmk[:, 0:512], in0=qmk[:, 0:512],
                                          in1=qmk2[:, 0:512], op=ALU.logical_or))
                ph.phase(vector=(pho, 1))

                def phs(eng, inc, j=j):
                    inc(eng.select(out=sh8[:, 512 * j:512 * (j + 1)],
                                   mask=qmk[:, 0:512], on_true=qub[:, 0:512],
                                   on_false=qua[:, 0:512]))
                ph.phase(vector=(phs, 1))

            for cP in range(4):
                for mj in range(NFI):
                    s = mj % 2

                    def pwl(eng, inc, mj=mj, cP=cP, s=s):
                        inc(eng.dma_start(out=swp[:, 512 * s:512 * (s + 1)],
                                          in_=wproj8[128 * mj:128 * (mj + 1),
                                                     512 * cP:512 * (cP + 1)]), 16)
                    ph.phase(sync=(pwl, 16))

                    def pfm(eng, inc, mj=mj, s=s):
                        for t in range(4):
                            mm = eng.matmul(pW[:, 512 * t:512 * (t + 1)],
                                            sh8[:, 512 * mj + 128 * t:
                                                 512 * mj + 128 * (t + 1)],
                                            swp[:, 512 * s:512 * (s + 1)],
                                            start=(mj == 0), stop=(mj == NFI - 1))
                        inc(mm)
                    ph.phase(tensor=(pfm, 1))

                for t in range(4):
                    tg = 4 * half + t

                    def psc(eng, inc, t=t):
                        inc(eng.tensor_scalar(out=stp[:], in0=pW[:, 512 * t:512 * (t + 1)],
                                              scalar1=DQ2, scalar2=None, op0=ALU.mult))
                    ph.phase(vector=(psc, 1))

                    def pad(eng, inc, tg=tg, cP=cP):
                        sl = sR1[:, D * tg + 512 * cP:D * tg + 512 * (cP + 1)]
                        inc(eng.tensor_add(out=sl, in0=sl, in1=stp[:]))
                    ph.phase(vector=(pad, 1))

        def pst(eng, inc):
            _dma_tiles(eng, inc, sR1, x1, TPC, D, to_sbuf=False)
        ph.phase(gpsimd=(pst, 16 * NT))

        ph.finish(ctx)
    return nc


def run_fc0(shards, consts, prep, allmax):
    wfc8 = np.concatenate(prep["wfc8_sh"], axis=0)
    wfc8t = np.ascontiguousarray(wfc8.reshape(D, NFI, 128).transpose(1, 0, 2))
    nc = _cached("fc0", prog_fc0)
    in_maps = [{"x8f": prep["x8f"][c], "wfc8t": wfc8t, "allmax": allmax,
                "identF": consts["identF"]} for c in range(NCORES)]
    res = run_bass_kernel_spmd(nc, in_maps, core_ids=list(range(NCORES)))
    h = [res.results[c]["hbuf"] for c in range(NCORES)]
    allhmax = np.concatenate([res.results[c]["hmax"][0] for c in range(NCORES)]
                             ).reshape(1, 8)
    return h, allhmax


def run_proj0(shards, consts, prep, comb_own, h, allmax, allhmax):
    wproj8 = np.concatenate(prep["wproj8_sh"], axis=0)
    nc = _cached("proj0", prog_proj0)
    in_maps = []
    for c in range(NCORES):
        in_maps.append({
            "x": shards["x_slab"][c], "combd": comb_own[c], "woT": shards["woT0"],
            "hbuf": h[c], "wproj8": wproj8, "allmax": allmax, "allhmax": allhmax,
        })
    res = run_bass_kernel_spmd(nc, in_maps, core_ids=list(range(NCORES)))
    return [res.results[c]["x1"] for c in range(NCORES)]



def prog_qkv1():
    nc = bass.Bass("TRN2", target_bir_lowering=False)
    xn1 = nc.dram_tensor("xn1", [TPC, D], F32, kind="ExternalInput").ap()
    wq = nc.dram_tensor("wq", [NF, D, 128], F32, kind="ExternalInput").ap()
    wk = nc.dram_tensor("wk", [NF, D, 128], F32, kind="ExternalInput").ap()
    wv = nc.dram_tensor("wv", [NF, D, 128], F32, kind="ExternalInput").ap()
    identB = nc.dram_tensor("identB", [128, 128], BF16, kind="ExternalInput").ap()
    q1 = nc.dram_tensor("q1", [D, TPC], BF16, kind="ExternalOutput").ap()
    k1 = nc.dram_tensor("k1", [D, TPC], BF16, kind="ExternalOutput").ap()
    v1 = nc.dram_tensor("v1", [D, TPC], BF16, kind="ExternalOutput").ap()

    with contextlib.ExitStack() as ctx:
        sXnF = ctx.enter_context(nc.sbuf_tensor("sXnF", [128, NF * TPC], BF16))
        sxt = ctx.enter_context(nc.sbuf_tensor("sxt", [128, 2 * D], F32))
        sxb = ctx.enter_context(nc.sbuf_tensor("sxb", [128, 2 * D], BF16))
        sidB = ctx.enter_context(nc.sbuf_tensor("sidB", [128, 128], BF16))
        swf = ctx.enter_context(nc.sbuf_tensor("swf", [128, 2 * D], F32))
        swb = ctx.enter_context(nc.sbuf_tensor("swb", [128, 2 * D], BF16))
        sQa = ctx.enter_context(nc.sbuf_tensor("sQa", [128, 2 * TPC], BF16))
        ptr = ctx.enter_context(nc.psum_tensor("ptr", [128, 512], BF16))
        pa = ctx.enter_context(nc.psum_tensor("pa", [128, 2 * TPC], F32))

        ph = Phaser(nc)

        def p0(eng, inc):
            inc(eng.dma_start(out=sidB[:], in_=identB), 16)
        ph.phase(sync=(p0, 16))

        add_tm2fm_phases(ph, xn1, sXnF, sxt, sxb, ptr, sidB)

        for wi, (wten, outd, scale) in enumerate(
                ((wq, q1, SM_SCALE1), (wk, k1, None), (wv, v1, None))):
            for m in range(NF):
                s = m % 2

                def plw(eng, inc, wten=wten, m=m, s=s):
                    for j in range(NF):
                        inc(eng.dma_start(
                            out=swf[:, D * s + 128 * j:D * s + 128 * (j + 1)],
                            in_=wten[m, 128 * j:128 * (j + 1), :]), 16)
                ph.phase(sync=(plw, 16 * NF))

                def pcw(eng, inc, s=s):
                    inc(eng.activation(out=swb[:, D * s:D * (s + 1)],
                                       in_=swf[:, D * s:D * (s + 1)], func=AF.Copy))
                ph.phase(scalar=(pcw, 1))

                def pmm(eng, inc, s=s):
                    for h in range(2):
                        for j in range(NF):
                            mm = eng.matmul(
                                pa[:, TPC * s + 512 * h:TPC * s + 512 * (h + 1)],
                                swb[:, D * s + 128 * j:D * s + 128 * (j + 1)],
                                sXnF[:, TPC * j + 512 * h:TPC * j + 512 * (h + 1)],
                                start=(j == 0), stop=(j == NF - 1))
                        inc(mm)
                ph.phase(tensor=(pmm, 2))

                def pev(eng, inc, s=s, scale=scale):
                    if scale is None:
                        inc(eng.activation(out=sQa[:, TPC * s:TPC * (s + 1)],
                                           in_=pa[:, TPC * s:TPC * (s + 1)],
                                           func=AF.Copy))
                    else:
                        inc(eng.activation(out=sQa[:, TPC * s:TPC * (s + 1)],
                                           in_=pa[:, TPC * s:TPC * (s + 1)],
                                           func=AF.Identity, scale=scale))
                ph.phase(scalar=(pev, 1))

                def pso(eng, inc, outd=outd, m=m, s=s):
                    inc(eng.dma_start(out=outd[128 * m:128 * (m + 1), :],
                                      in_=sQa[:, TPC * s:TPC * (s + 1)]), 16)
                ph.phase(gpsimd=(pso, 16))

        ph.finish(ctx)
    return nc



def prog_attn1():
    nc = bass.Bass("TRN2", target_bir_lowering=False)
    q1 = nc.dram_tensor("q1", [D, TPC], BF16, kind="ExternalInput").ap()
    k1f = nc.dram_tensor("k1f", [D, SEQ], BF16, kind="ExternalInput").ap()
    v1t = nc.dram_tensor("v1t", [SEQ, D], BF16, kind="ExternalInput").ap()
    woT = nc.dram_tensor("woT", [D, D], F32, kind="ExternalInput").ap()
    bo_row = nc.dram_tensor("bo_row", [1, D], F32, kind="ExternalInput").ap()
    x1 = nc.dram_tensor("x1", [TPC, D], F32, kind="ExternalInput").ap()
    identB = nc.dram_tensor("identB", [128, 128], BF16, kind="ExternalInput").ap()
    identF = nc.dram_tensor("identF", [128, 128], F32, kind="ExternalInput").ap()
    r2 = nc.dram_tensor("r2", [TPC, D], F32, kind="ExternalOutput").ap()

    with contextlib.ExitStack() as ctx:
        sQ1 = ctx.enter_context(nc.sbuf_tensor("sQ1", [128, NF * TPC], BF16))
        sK1 = ctx.enter_context(nc.sbuf_tensor("sK1", [128, NF * SEQ], BF16))
        sPT = ctx.enter_context(nc.sbuf_tensor("sPT", [128, NF * TPC], BF16))
        sidB = ctx.enter_context(nc.sbuf_tensor("sidB", [128, 128], BF16))
        sidF = ctx.enter_context(nc.sbuf_tensor("sidF", [128, 128], F32))
        sneg = ctx.enter_context(nc.sbuf_tensor("sneg", [128, 8], F32))
        snegrb = ctx.enter_context(nc.sbuf_tensor("snegrb", [8, 128], BF16))
        snegrow = ctx.enter_context(nc.sbuf_tensor("snegrow", [1, TPC], BF16))
        sonerow = ctx.enter_context(nc.sbuf_tensor("sonerow", [1, 128], BF16))
        sonecol = ctx.enter_context(nc.sbuf_tensor("sonecol", [128, 1], BF16))
        srd = ctx.enter_context(nc.sbuf_tensor("srd", [128, 8], F32))
        svs = ctx.enter_context(nc.sbuf_tensor("svs", [128, 2 * 512], BF16))
        sbo = ctx.enter_context(nc.sbuf_tensor("sbo", [1, D], F32))
        sbob = ctx.enter_context(nc.sbuf_tensor("sbob", [1, D], BF16))
        swf = ctx.enter_context(nc.sbuf_tensor("swf", [128, 512], F32))
        swb = ctx.enter_context(nc.sbuf_tensor("swb", [128, 2 * 512], BF16))
        sxs = ctx.enter_context(nc.sbuf_tensor("sxs", [128, 2 * 512], F32))
        sout = ctx.enter_context(nc.sbuf_tensor("sout", [128, 2 * 512], F32))
        pS = ctx.enter_context(nc.psum_tensor("pS", [128, 2048], F32))
        pCT = ctx.enter_context(nc.psum_tensor("pCT", [128, 512], BF16))

        sAT = sK1
        sAF = sQ1

        ph = Phaser(nc)

        def p0(eng, inc):
            inc(eng.dma_start(out=sidB[:], in_=identB), 16)
            inc(eng.dma_start(out=sidF[:], in_=identF), 16)
            inc(eng.dma_start(out=sbo[:], in_=bo_row), 16)
            _dma_tiles(eng, inc, sQ1, q1, D, TPC)
            _dma_tiles(eng, inc, sK1, k1f, D, SEQ)
        ph.phase(sync=(p0, 16 * 35))

        def p1(eng, inc):
            eng.memset(sonerow[:], 1.0)
            inc(eng.memset(sonecol[:], 1.0))
            inc(eng.tensor_copy(out=sbob[:], in_=sbo[:]))
        ph.phase(vector=(p1, 2))

        for u in range(8):
            def d1(eng, inc, u=u):
                for kc in range(4):
                    for j in range(NF):
                        mm = eng.matmul(pS[:, 512 * kc:512 * (kc + 1)],
                                        sQ1[:, TPC * j + 128 * u:TPC * j + 128 * (u + 1)],
                                        sK1[:, SEQ * j + 512 * kc:SEQ * j + 512 * (kc + 1)],
                                        start=(j == 0), stop=(j == NF - 1))
                inc(mm)
            ph.phase(tensor=(d1, 1))

            def d2(eng, inc, u=u):
                inc(eng.tensor_reduce(out=sneg[:, u:u + 1], in_=pS[:], axis=AX.X,
                                      op=ALU.max, negate=True))
            ph.phase(vector=(d2, 1))

        def e1(eng, inc):
            inc(eng.transpose(pS[0:8, 0:128], sneg[:], sidF[:]))
        ph.phase(tensor=(e1, 1))

        def e2(eng, inc):
            inc(eng.tensor_copy(out=snegrb[:], in_=pS[0:8, 0:128]))
        ph.phase(vector=(e2, 1))

        def e3(eng, inc):
            for r in range(8):
                inc(eng.dma_start(out=snegrow[0:1, 128 * r:128 * (r + 1)],
                                  in_=snegrb[r:r + 1, :]), 16)
        ph.phase(gpsimd=(e3, 128))

        for kb in range(NF):
            for qh2 in range(2):
                def f1(eng, inc, kb=kb, qh2=qh2):
                    p = pS[:, 512 * (qh2 % 2):512 * (qh2 % 2) + 512]
                    for j in range(NF):
                        eng.matmul(p, sK1[:, SEQ * j + 128 * kb:SEQ * j + 128 * (kb + 1)],
                                   sQ1[:, TPC * j + 512 * qh2:TPC * j + 512 * (qh2 + 1)],
                                   start=(j == 0), stop=False)
                    inc(eng.matmul(p, sonerow[:],
                                   snegrow[0:1, 512 * qh2:512 * (qh2 + 1)],
                                   start=False, stop=True))
                ph.phase(tensor=(f1, 1))

                def f2(eng, inc, kb=kb, qh2=qh2):
                    inc(eng.activation(
                        out=sPT[:, TPC * kb + 512 * qh2:TPC * kb + 512 * (qh2 + 1)],
                        in_=pS[:, 512 * (qh2 % 2):512 * (qh2 % 2) + 512], func=AF.Exp))
                ph.phase(scalar=(f2, 1))

        for u in range(8):
            def g1(eng, inc, u=u):
                for kb in range(NF):
                    mm = eng.matmul(pS[:, 1536 + u:1537 + u],
                                    sPT[:, TPC * kb + 128 * u:TPC * kb + 128 * (u + 1)],
                                    sonecol[:], start=(kb == 0), stop=(kb == NF - 1))
                inc(mm)
            ph.phase(tensor=(g1, 1))

        def g2(eng, inc):
            for u in range(8):
                inc(eng.reciprocal(out=srd[:, u:u + 1], in_=pS[:, 1536 + u:1537 + u]))
        ph.phase(vector=(g2, 8))

        for up in range(2):
            for cE in range(4):
                for kb in range(NF):
                    s = kb % 2

                    def h1(eng, inc, kb=kb, cE=cE, s=s):
                        inc(eng.dma_start(out=svs[:, 512 * s:512 * (s + 1)],
                                          in_=v1t[128 * kb:128 * (kb + 1),
                                                  512 * cE:512 * (cE + 1)]), 16)
                    ph.phase(sync=(h1, 16))

                    def h2(eng, inc, kb=kb, up=up, s=s):
                        for u4 in range(4):
                            u = 4 * up + u4
                            mm = eng.matmul(pS[:, 512 * u4:512 * (u4 + 1)],
                                            sPT[:, TPC * kb + 128 * u:
                                                 TPC * kb + 128 * (u + 1)],
                                            svs[:, 512 * s:512 * (s + 1)],
                                            start=(kb == 0), stop=(kb == NF - 1))
                        inc(mm)
                    ph.phase(tensor=(h2, 1))

                def h3(eng, inc, up=up, cE=cE):
                    for u4 in range(4):
                        u = 4 * up + u4
                        inc(eng.activation(
                            out=sAT[:, SEQ * u + 512 * cE:SEQ * u + 512 * (cE + 1)],
                            in_=pS[:, 512 * u4:512 * (u4 + 1)],
                            func=AF.Identity, scale=srd[:, u:u + 1]))
                ph.phase(scalar=(h3, 4))

        for u in range(8):
            for g4 in range(4):
                def t1(eng, inc, u=u, g4=g4):
                    for w in range(4):
                        j = 4 * g4 + w
                        inc(eng.transpose(pCT[:, 128 * w:128 * (w + 1)],
                                          sAT[:, SEQ * u + 128 * j:SEQ * u + 128 * (j + 1)],
                                          sidB[:]))
                ph.phase(tensor=(t1, 4))

                def t2(eng, inc, u=u, g4=g4):
                    for w in range(4):
                        j = 4 * g4 + w
                        inc(eng.tensor_copy(
                            out=sAF[:, TPC * j + 128 * u:TPC * j + 128 * (u + 1)],
                            in_=pCT[:, 128 * w:128 * (w + 1)]))
                ph.phase(vector=(t2, 4))

        for tp in range(2):
            for cP in range(4):
                for j in range(NF):
                    s = j % 2

                    def w1(eng, inc, j=j, cP=cP):
                        inc(eng.dma_start(out=swf[:],
                                          in_=woT[128 * j:128 * (j + 1),
                                                  512 * cP:512 * (cP + 1)]), 16)
                    ph.phase(sync=(w1, 16))

                    def w2(eng, inc, s=s):
                        inc(eng.activation(out=swb[:, 512 * s:512 * (s + 1)],
                                           in_=swf[:], func=AF.Copy))
                    ph.phase(scalar=(w2, 1))

                    def w3(eng, inc, j=j, tp=tp, s=s, cP=cP):
                        for t4 in range(4):
                            t = 4 * tp + t4
                            mm = eng.matmul(pS[:, 512 * t4:512 * (t4 + 1)],
                                            sAF[:, TPC * j + 128 * t:TPC * j + 128 * (t + 1)],
                                            swb[:, 512 * s:512 * (s + 1)],
                                            start=(j == 0), stop=False)
                        if j == NF - 1:
                            for t4 in range(4):
                                mm = eng.matmul(pS[:, 512 * t4:512 * (t4 + 1)],
                                                sonerow[:],
                                                sbob[0:1, 512 * cP:512 * (cP + 1)],
                                                start=False, stop=True)
                        inc(mm)
                    ph.phase(tensor=(w3, 1))

                for t4 in range(4):
                    t = 4 * tp + t4

                    def w4(eng, inc, t=t, cP=cP, t4=t4):
                        inc(eng.dma_start(out=sxs[:, 512 * (t4 % 2):512 * (t4 % 2) + 512],
                                          in_=x1[128 * t:128 * (t + 1),
                                                 512 * cP:512 * (cP + 1)]), 16)
                    ph.phase(sync=(w4, 16))

                    def w5(eng, inc, t4=t4):
                        inc(eng.tensor_add(out=sout[:, 512 * (t4 % 2):512 * (t4 % 2) + 512],
                                           in0=sxs[:, 512 * (t4 % 2):512 * (t4 % 2) + 512],
                                           in1=pS[:, 512 * t4:512 * (t4 + 1)]))
                    ph.phase(vector=(w5, 1))

                    def w6(eng, inc, t=t, cP=cP, t4=t4):
                        inc(eng.dma_start(out=r2[128 * t:128 * (t + 1),
                                                 512 * cP:512 * (cP + 1)],
                                          in_=sout[:, 512 * (t4 % 2):512 * (t4 % 2) + 512]), 16)
                    ph.phase(gpsimd=(w6, 16))

        ph.finish(ctx)
    return nc



def prog_lut():
    nc = bass.Bass("TRN2", target_bir_lowering=False)
    lut_row = nc.dram_tensor("lut_row", [1, 16], F32, kind="ExternalInput").ap()
    gidx = nc.dram_tensor("gidx", [256, INTER1], I32, kind="ExternalInput").ap()
    uidx = nc.dram_tensor("uidx", [256, INTER1], I32, kind="ExternalInput").ap()
    didx = nc.dram_tensor("didx", [768, D], I32, kind="ExternalInput").ap()
    wg_sh = nc.dram_tensor("wg_sh", [256, INTER1], BF16, kind="ExternalOutput").ap()
    wu_sh = nc.dram_tensor("wu_sh", [256, INTER1], BF16, kind="ExternalOutput").ap()
    wd_sh = nc.dram_tensor("wd_sh", [768, D], BF16, kind="ExternalOutput").ap()

    C = 1376
    with contextlib.ExitStack() as ctx:
        sI = ctx.enter_context(nc.sbuf_tensor("sI", [128, D], I32))
        sB = ctx.enter_context(nc.sbuf_tensor("sB", [128, D], BF16))
        sT = ctx.enter_context(nc.sbuf_tensor("sT", [128, D], BF16))
        sAa = ctx.enter_context(nc.sbuf_tensor("sAa", [128, D], BF16))
        sAb = ctx.enter_context(nc.sbuf_tensor("sAb", [128, D], BF16))
        sone = ctx.enter_context(nc.sbuf_tensor("sone", [1, 128], F32))
        slr = ctx.enter_context(nc.sbuf_tensor("slr", [1, 16], F32))
        slc = ctx.enter_context(nc.sbuf_tensor("slc", [128, 16], F32))
        pRep = ctx.enter_context(nc.psum_tensor("pRep", [128, 16], F32))

        ph = Phaser(nc)

        def p0(eng, inc):
            inc(eng.dma_start(out=slr[:], in_=lut_row), 16)
        ph.phase(sync=(p0, 16))

        def p1(eng, inc):
            inc(eng.memset(sone[:], 1.0))
        ph.phase(vector=(p1, 1))

        def p2(eng, inc):
            inc(eng.matmul(pRep[:], sone[:], slr[:], start=True, stop=True))
        ph.phase(tensor=(p2, 1))

        def p3(eng, inc):
            inc(eng.tensor_copy(out=slc[:], in_=pRep[:]))
        ph.phase(vector=(p3, 1))

        jobs = []
        for t in range(2):
            for cc in range(4):
                jobs.append((gidx, wg_sh, 128 * t, C * cc, C))
                jobs.append((uidx, wu_sh, 128 * t, C * cc, C))
        for t in range(6):
            jobs.append((didx, wd_sh, 128 * t, 0, D))

        for (srcd, dstd, r0, c0, W) in jobs:
            def jl(eng, inc, srcd=srcd, r0=r0, c0=c0, W=W):
                inc(eng.dma_start(out=sI[:, 0:W],
                                  in_=srcd[r0:r0 + 128, c0:c0 + W]), 16)
            ph.phase(sync=(jl, 16))

            def jc(eng, inc, W=W):
                inc(eng.tensor_copy(out=sB[:, 0:W], in_=sI[:, 0:W]))
            ph.phase(vector=(jc, 1))

            def j0(eng, inc, W=W):
                inc(eng.tensor_scalar(out=sAa[:, 0:W], in0=sB[:, 0:W],
                                      scalar1=0.0, scalar2=slc[:, 0:1],
                                      op0=ALU.is_equal, op1=ALU.mult))
            ph.phase(vector=(j0, 1))
            cur, oth = sAa, sAb
            for l in range(1, 16):
                def jt(eng, inc, l=l, W=W):
                    inc(eng.tensor_scalar(out=sT[:, 0:W], in0=sB[:, 0:W],
                                          scalar1=float(l), scalar2=slc[:, l:l + 1],
                                          op0=ALU.is_equal, op1=ALU.mult))
                ph.phase(vector=(jt, 1))

                def ja(eng, inc, cur=cur, oth=oth, W=W):
                    inc(eng.tensor_add(out=oth[:, 0:W], in0=cur[:, 0:W],
                                       in1=sT[:, 0:W]))
                ph.phase(vector=(ja, 1))
                cur, oth = oth, cur

            def js(eng, inc, dstd=dstd, r0=r0, c0=c0, cur=cur, W=W):
                inc(eng.dma_start(out=dstd[r0:r0 + 128, c0:c0 + W],
                                  in_=cur[:, 0:W]), 16)
            ph.phase(gpsimd=(js, 16))

        ph.finish(ctx)
    return nc



def prog_ffn1():
    nc = bass.Bass("TRN2", target_bir_lowering=False)
    xn1 = nc.dram_tensor("xn1", [TPC, D], F32, kind="ExternalInput").ap()
    wgt = nc.dram_tensor("wgt", [NI1, D, 128], BF16, kind="ExternalInput").ap()
    wut = nc.dram_tensor("wut", [NI1, D, 128], BF16, kind="ExternalInput").ap()
    wd = nc.dram_tensor("wd", [INTER1, D], BF16, kind="ExternalInput").ap()
    r2 = nc.dram_tensor("r2", [TPC, D], F32, kind="ExternalInput").ap()
    identB = nc.dram_tensor("identB", [128, 128], BF16, kind="ExternalInput").ap()
    outd = nc.dram_tensor("outd", [TPC, D], F32, kind="ExternalOutput").ap()

    with contextlib.ExitStack() as ctx:
        sXnF = ctx.enter_context(nc.sbuf_tensor("sXnF", [128, NF * TPC], BF16))
        sxt = ctx.enter_context(nc.sbuf_tensor("sxt", [128, 2 * D], F32))
        sxb = ctx.enter_context(nc.sbuf_tensor("sxb", [128, 2 * D], BF16))
        sidB = ctx.enter_context(nc.sbuf_tensor("sidB", [128, 128], BF16))
        sH1 = ctx.enter_context(nc.sbuf_tensor("sH1", [128, NI1 * TPC], BF16))
        swg = ctx.enter_context(nc.sbuf_tensor("swg", [128, 2 * D], BF16))
        swu = ctx.enter_context(nc.sbuf_tensor("swu", [128, 2 * D], BF16))
        ssil = ctx.enter_context(nc.sbuf_tensor("ssil", [128, TPC], F32))
        swd = ctx.enter_context(nc.sbuf_tensor("swd", [128, 2 * 512], BF16))
        sxs = ctx.enter_context(nc.sbuf_tensor("sxs", [128, 2 * 512], F32))
        sout = ctx.enter_context(nc.sbuf_tensor("sout", [128, 2 * 512], F32))
        ptr = ctx.enter_context(nc.psum_tensor("ptr", [128, 512], BF16))
        pBig = ctx.enter_context(nc.psum_tensor("pBig", [128, 2048], F32))


        ph = Phaser(nc)

        def p0(eng, inc):
            inc(eng.dma_start(out=sidB[:], in_=identB), 16)
        ph.phase(sync=(p0, 16))

        add_tm2fm_phases(ph, xn1, sXnF, sxt, sxb, ptr, sidB)

        for m in range(NI1):
            s = m % 2

            def gl(eng, inc, m=m, s=s):
                for j in range(NF):
                    inc(eng.dma_start(out=swg[:, D * s + 128 * j:D * s + 128 * (j + 1)],
                                      in_=wgt[m, 128 * j:128 * (j + 1), :]), 16)
                for j in range(NF):
                    inc(eng.dma_start(out=swu[:, D * s + 128 * j:D * s + 128 * (j + 1)],
                                      in_=wut[m, 128 * j:128 * (j + 1), :]), 16)
            ph.phase(sync=(gl, 16 * 2 * NF))

            def gm(eng, inc, m=m, s=s):
                for h in range(2):
                    for j in range(NF):
                        mm = eng.matmul(pBig[:, 512 * h:512 * (h + 1)],
                                        swg[:, D * s + 128 * j:D * s + 128 * (j + 1)],
                                        sXnF[:, TPC * j + 512 * h:TPC * j + 512 * (h + 1)],
                                        start=(j == 0), stop=(j == NF - 1))
                    inc(mm)
                for h in range(2):
                    for j in range(NF):
                        mm = eng.matmul(pBig[:, TPC + 512 * h:TPC + 512 * (h + 1)],
                                        swu[:, D * s + 128 * j:D * s + 128 * (j + 1)],
                                        sXnF[:, TPC * j + 512 * h:TPC * j + 512 * (h + 1)],
                                        start=(j == 0), stop=(j == NF - 1))
                    inc(mm)
            ph.phase(tensor=(gm, 4))

            def gs(eng, inc):
                inc(eng.activation(out=ssil[:], in_=pBig[:, 0:TPC], func=AF.Silu))
            ph.phase(scalar=(gs, 1))

            def gp(eng, inc, m=m):
                inc(eng.tensor_mul(out=sH1[:, TPC * m:TPC * (m + 1)],
                                   in0=ssil[:], in1=pBig[:, TPC:2 * TPC]))
            ph.phase(vector=(gp, 1))

        for tp in range(2):
            for cP in range(4):
                for mj in range(NI1):
                    s = mj % 2

                    def dl(eng, inc, mj=mj, cP=cP, s=s):
                        inc(eng.dma_start(out=swd[:, 512 * s:512 * (s + 1)],
                                          in_=wd[128 * mj:128 * (mj + 1),
                                                 512 * cP:512 * (cP + 1)]), 16)
                    ph.phase(sync=(dl, 16))

                    def dm(eng, inc, mj=mj, tp=tp, s=s):
                        for t4 in range(4):
                            t = 4 * tp + t4
                            mm = eng.matmul(pBig[:, 512 * t4:512 * (t4 + 1)],
                                            sH1[:, TPC * mj + 128 * t:
                                                 TPC * mj + 128 * (t + 1)],
                                            swd[:, 512 * s:512 * (s + 1)],
                                            start=(mj == 0), stop=(mj == NI1 - 1))
                        inc(mm)
                    ph.phase(tensor=(dm, 1))

                for t4 in range(4):
                    t = 4 * tp + t4

                    def dx(eng, inc, t=t, cP=cP, t4=t4):
                        inc(eng.dma_start(out=sxs[:, 512 * (t4 % 2):512 * (t4 % 2) + 512],
                                          in_=r2[128 * t:128 * (t + 1),
                                                 512 * cP:512 * (cP + 1)]), 16)
                    ph.phase(sync=(dx, 16))

                    def dv(eng, inc, t4=t4):
                        inc(eng.tensor_add(out=sout[:, 512 * (t4 % 2):512 * (t4 % 2) + 512],
                                           in0=sxs[:, 512 * (t4 % 2):512 * (t4 % 2) + 512],
                                           in1=pBig[:, 512 * t4:512 * (t4 + 1)]))
                    ph.phase(vector=(dv, 1))

                    def ds(eng, inc, t=t, cP=cP, t4=t4):
                        inc(eng.dma_start(out=outd[128 * t:128 * (t + 1),
                                                   512 * cP:512 * (cP + 1)],
                                          in_=sout[:, 512 * (t4 % 2):512 * (t4 % 2) + 512]), 16)
                    ph.phase(gpsimd=(ds, 16))

        ph.finish(ctx)
    return nc


def run_qkv1(shards, consts, xn1):
    nc = _cached("qkv1", prog_qkv1)
    in_maps = [{"xn1": xn1[c], "wq": shards["wq1"], "wk": shards["wk1"],
                "wv": shards["wv1"], "identB": consts["identB"]}
               for c in range(NCORES)]
    res = run_bass_kernel_spmd(nc, in_maps, core_ids=list(range(NCORES)))
    return {k: [res.results[c][k] for c in range(NCORES)] for k in ("q1", "k1", "v1")}


def run_attn1(shards, consts, qkv, x1):
    nc = _cached("attn1", prog_attn1)
    in_maps = []
    for c in range(NCORES):
        b = c // 2
        k1f = np.concatenate([qkv["k1"][2 * b], qkv["k1"][2 * b + 1]], axis=1)
        v1f = np.concatenate([qkv["v1"][2 * b], qkv["v1"][2 * b + 1]], axis=1)
        v1t = np.ascontiguousarray(v1f.T)
        in_maps.append({
            "q1": qkv["q1"][c], "k1f": np.ascontiguousarray(k1f), "v1t": v1t,
            "woT": shards["woT1"], "bo_row": shards["bo_row"], "x1": x1[c],
            "identB": consts["identB"], "identF": consts["identF"],
        })
    res = run_bass_kernel_spmd(nc, in_maps, core_ids=list(range(NCORES)))
    return [res.results[c]["r2"] for c in range(NCORES)]


def run_lut(shards):
    nc = _cached("lut", prog_lut)
    dT = np.concatenate(shards["didxT_sh"], axis=0)
    dTp = np.concatenate([dT, np.zeros((6144 - INTER1, D), np.int32)], axis=0)
    in_maps = []
    for c in range(NCORES):
        in_maps.append({
            "lut_row": shards["lut_row"],
            "gidx": shards["gidxT_sh"][c], "uidx": shards["uidxT_sh"][c],
            "didx": np.ascontiguousarray(dTp[768 * c:768 * (c + 1)]),
        })
    res = run_bass_kernel_spmd(nc, in_maps, core_ids=list(range(NCORES)))
    wg = np.concatenate([res.results[c]["wg_sh"] for c in range(NCORES)], axis=0)
    wu = np.concatenate([res.results[c]["wu_sh"] for c in range(NCORES)], axis=0)
    wdp = np.concatenate([res.results[c]["wd_sh"] for c in range(NCORES)], axis=0)
    return wg, wu, wdp[:INTER1]


def run_ffn1(shards, consts, xn1, r2, wg, wu, wd):
    wgt = np.ascontiguousarray(wg.reshape(D, NI1, 128).transpose(1, 0, 2))
    wut = np.ascontiguousarray(wu.reshape(D, NI1, 128).transpose(1, 0, 2))
    nc = _cached("ffn1", prog_ffn1)
    in_maps = [{"xn1": xn1[c], "wgt": wgt, "wut": wut, "wd": np.ascontiguousarray(wd),
                "r2": r2[c], "identB": consts["identB"]} for c in range(NCORES)]
    res = run_bass_kernel_spmd(nc, in_maps, core_ids=list(range(NCORES)))
    return [res.results[c]["outd"] for c in range(NCORES)]


def kernel(**inputs):
    consts = host_constants()
    shards = host_shards(inputs)
    xn, allmax = run_ln(shards, consts, with_maxes=True)
    prep = run_prep0(shards, consts, xn, allmax)
    hs = host_attn_shuffle(prep)
    comb = run_attn0(shards, consts, hs)
    comb_own = host_comb_shuffle(comb)
    h, allhmax = run_fc0(shards, consts, prep, allmax)
    x1 = run_proj0(shards, consts, prep, comb_own, h, allmax, allhmax)
    xn1, _ = run_ln({"x_slab": x1, "wfcT_sh": None, "wprojT_sh": None} | {
        k: v for k, v in shards.items() if k != "x_slab"}, consts, with_maxes=False)
    qkv = run_qkv1(shards, consts, xn1)
    r2 = run_attn1(shards, consts, qkv, x1)
    wg, wu, wd = run_lut(shards)
    out = run_ffn1(shards, consts, xn1, r2, wg, wu, wd)
    full = np.concatenate(out, axis=0)
    return full.reshape(4, SEQ, D)
